# version 02
# baseline (speedup 1.0000x reference)
from collections import defaultdict
from contextlib import ExitStack

import numpy as np
import ml_dtypes
import concourse.bass as bass
import concourse.mybir as mybir
from concourse.bass_utils import run_bass_kernel_spmd

F32 = mybir.dt.float32
BF16 = mybir.dt.bfloat16
I16 = mybir.dt.int16
I32 = mybir.dt.int32
AF = mybir.ActivationFunctionType
ALU = mybir.AluOpType
AXX = mybir.AxisListType.X
AP = bass.AP

H = 2048
HI = 16
NCORE = 8
TPC = 2048
NBLK = TPC // 512
TB = 512
TBP = 528
G = 2
NG = HI // G
NTT = TB // 128
FFN_O = 5504
OB = FFN_O // 128
LN_EPS = 1e-5
TILE = 128 * 128
FFN_TQ = 1024
NHALF = TPC // FFN_TQ

BLK2 = 128 * 256
A_SZ = 4 * 16 * BLK2
G_BASE = A_SZ
GU_SZ = OB * BLK2
U_BASE = G_BASE + GU_SZ
D_BASE = U_BASE + GU_SZ
D_SZ = OB * BLK2
RANK_SZ = D_BASE + D_SZ


class Prog:
    def __init__(self):
        self.events = []
        self.cnt = defaultdict(int)

    def ev(self, eng, fn, waits=(), inc=None, n=1):
        rw = [(s, (self.cnt[s] if v is None else v)) for (s, v) in waits]
        rw = [(s, v) for (s, v) in rw if v > 0]
        if inc is not None:
            self.cnt[inc] += n
        self.events.append((eng, fn, rw, inc, n))

    def barrier(self):
        bw = [(s, None) for s in list(self.cnt.keys())]
        for eng in ("sync", "gpsimd", "vector", "scalar", "tensor"):
            self.ev(eng, None, waits=bw)

    def emit(self, eng_name, eng, sems):
        last = {}
        for (e, fn, rw, inc, n) in self.events:
            if e != eng_name:
                continue
            for (s, v) in rw:
                if last.get(s, 0) < v:
                    eng.wait_ge(sems[s], v)
                    last[s] = v
            inst = fn(eng) if fn is not None else None
            if inc is not None:
                assert inst is not None
                inst.then_inc(sems[inc], n)


def build_program(dbg=False):
    nc = bass.Bass("TRN2", target_bir_lowering=False, debug=False)

    xh = nc.dram_tensor("xh", [TPC + 1, H], F32, kind="ExternalInput").ap()
    wsl = nc.dram_tensor("wsl", [4, 256, H], F32, kind="ExternalInput").ap()
    vecs = nc.dram_tensor("vecs", [7, 128, HI], F32, kind="ExternalInput").ap()
    grep_in = nc.dram_tensor("grep_in", [128, H], BF16, kind="ExternalInput").ap()
    brep_in = nc.dram_tensor("brep_in", [128, H], BF16, kind="ExternalInput").ap()
    lutb_in = nc.dram_tensor("lutb_in", [128, 16], F32, kind="ExternalInput").ap()
    maskv_in = nc.dram_tensor("maskv_in", [128, 1], F32, kind="ExternalInput").ap()
    ramp0_in = nc.dram_tensor("ramp0_in", [128, TB], F32, kind="ExternalInput").ap()
    gidx = nc.dram_tensor("gidx", [FFN_O, 256], I32, kind="ExternalInput").ap()
    uidx = nc.dram_tensor("uidx", [FFN_O, 256], I32, kind="ExternalInput").ap()
    didx = nc.dram_tensor("didx", [256, FFN_O], I32, kind="ExternalInput").ap()
    y = nc.dram_tensor("y", [TPC, H], F32, kind="ExternalOutput").ap()

    cc_w_in = nc.dram_tensor("cc_w_in", [RANK_SZ], BF16)
    cc_w_out = nc.dram_tensor("cc_w_out", [NCORE * RANK_SZ], BF16)

    def idram(name, shape, dt):
        kind = "ExternalOutput" if dbg else "Internal"
        if dbg:
            return nc.dram_tensor(name, shape, dt, kind="ExternalOutput")
        return nc.dram_tensor(name, shape, dt)

    spA = idram("spA", [NBLK, HI, 128, TB + 1], F32)
    spB = idram("spB", [NBLK, HI, 128, TB + 1], F32)
    spF = idram("spF", [NBLK, HI, 128, TB], F32)
    spFV = idram("spFV", [NBLK, HI, 128, TB], F32)
    spR = idram("spR", [NBLK, HI, 128, TB], BF16)
    spX = idram("spX", [TPC, H], F32)
    spXB = idram("spXB", [TPC, H], BF16)
    if dbg:
        spRW = idram("spRW", [NBLK, 128, HI, TB], BF16)
        spL2 = idram("spL2", [NBLK, 128, NTT, H], F32)
        spST = idram("spST", [128, 40], F32)
        spK = idram("spK", [NBLK, HI, 128, TB], F32)
        spV = idram("spV", [NBLK, HI, 128, TB], F32)
        spWKV = idram("spWKV", [NBLK, HI, 128, TB], F32)
        spWI = idram("spWI", [RANK_SZ], BF16)
        spWO = idram("spWO", [4 * RANK_SZ], BF16)
        spXN = idram("spXN", [2, 128, HI, TBP], BF16)
        spKX = idram("spKX", [128, HI, TB], BF16)

    p = Prog()
    SEM_NAMES = ["s_x", "s_tr", "s_gp", "s_mm", "s_ve", "s_ac", "s_cc",
                 "s_st", "s_sp", "s_xr"]

    with ExitStack() as ctx:
        def sb(name, shape, dt=F32):
            return ctx.enter_context(nc.sbuf_tensor(name, shape, dt))

        vec_s = sb("vec_s", [128, 7 * HI])
        mkv = vec_s[:, 2 * HI:3 * HI]; mvv = vec_s[:, 3 * HI:4 * HI]
        mrv = vec_s[:, 4 * HI:5 * HI]; uu = vec_s[:, 5 * HI:6 * HI]
        td = vec_s[:, 6 * HI:7 * HI]
        lutb = sb("lutb", [128, 16])
        maskv = sb("maskv", [128, 1])
        wdec = sb("wdec", [128, HI])
        ddec = sb("ddec", [128, HI])
        dec512 = sb("dec512", [128, HI])
        onemr = sb("onemr", [128, HI])
        pneg = sb("pneg", [128, NBLK * HI])
        ubias = sb("ubias", [128, NBLK * HI])
        aend = sb("aend", [128, HI]); bend = sb("bend", [128, HI])
        ainit = sb("ainit", [128, HI]); binit = sb("binit", [128, HI])
        ain = sb("ain", [128, HI]); bin_ = sb("bin_", [128, HI])
        tmp16 = sb("tmp16", [128, HI])
        tmp16b = sb("tmp16b", [128, HI])
        sx = sb("sx", [128, 4, 4])
        sy = sb("sy", [128, 4, 4])
        lnsx = sb("lnsx", [128, 4, 4])
        lnsy = sb("lnsy", [128, 4, 4])
        stlog = sb("stlog", [128, 40])
        epsv = sb("epsv", [128, 1])

        sems = {s: ctx.enter_context(nc.semaphore(s)) for s in SEM_NAMES}

        class Arena:
            def __init__(self, base):
                self.base = base
                self.off = base
                self.top = base

            def sbuf_tensor(self, name, shape, dt):
                per_part = int(np.prod(shape[1:])) * mybir.dt.size(dt)
                t = nc.alloc_sbuf_tensor_at(name, shape, dt, offset=self.off)
                self.off = (self.off + per_part + 31) & ~31
                self.top = max(self.top, self.off)
                return t

            def allocate(self):
                pass

            def free(self):
                pass

        ARENA_BYTES = 203 * 1024
        arena_backing = ctx.enter_context(
            nc.sbuf_tensor("arena_backing", [128, ARENA_BYTES], mybir.dt.uint8))
        ARENA_BASE = nc.lookup_mloc(arena_backing).addr
        LIMIT = ARENA_BASE + ARENA_BYTES

        arW = Arena(ARENA_BASE)
        idxs = arW.sbuf_tensor("idxs", [128, OB * 128], I16)
        idxT = arW.sbuf_tensor("idxT", [128, OB * 128], I16)
        wterm = [arW.sbuf_tensor(f"wterm{i}", [128, OB * 128], BF16) for i in range(2)]
        awt = arW.sbuf_tensor("awt", [128, H], F32)
        awb = arW.sbuf_tensor("awb", [128, H], BF16)
        assert arW.top <= LIMIT, arW.top

        ar1 = Arena(ARENA_BASE)
        xt = [ar1.sbuf_tensor(f"xt{i}", [128, H], F32) for i in range(4)]
        xnb = [ar1.sbuf_tensor(f"xnb{i}", [128, H], BF16) for i in range(4)]
        st1 = ar1.sbuf_tensor("st1", [128, H], F32)
        grep = ar1.sbuf_tensor("grep", [128, H], BF16)
        brep = ar1.sbuf_tensor("brep", [128, H], BF16)
        xncm = [ar1.sbuf_tensor(f"xncm{i}", [128, HI, TBP], BF16) for i in range(2)]
        xsc = ar1.sbuf_tensor("xsc", [128, HI, 128], BF16)
        kx = ar1.sbuf_tensor("kx", [128, HI, TB], BF16)
        vx = ar1.sbuf_tensor("vx", [128, HI, TB], BF16)
        wa = [ar1.sbuf_tensor(f"wa{i}", [128, HI * 128], BF16) for i in range(2)]
        kg = [ar1.sbuf_tensor(f"kg{i}", [128, G, TB], F32) for i in range(2)]
        vg = [ar1.sbuf_tensor(f"vg{i}", [128, G, TB], F32) for i in range(2)]
        sigr = ar1.sbuf_tensor("sigr", [128, HI, TB], BF16)
        Ee = ar1.sbuf_tensor("Ee", [128, G, TB], F32)
        Ff = ar1.sbuf_tensor("Ff", [128, G, TB], F32)
        FVv = ar1.sbuf_tensor("FVv", [128, G, TB], F32)
        scA = ar1.sbuf_tensor("scA", [128, G, TB + 1], F32)
        scB = ar1.sbuf_tensor("scB", [128, G, TB + 1], F32)
        assert ar1.top <= LIMIT, ar1.top

        ar2 = Arena(ARENA_BASE)
        Dsh = ar2.sbuf_tensor("Dsh", [128, HI, TB], F32)
        ramp0b = ar2.sbuf_tensor("ramp0b", [128, TB], F32)
        pA = ar2.sbuf_tensor("pA", [128, G, TB + 1], F32)
        pB = ar2.sbuf_tensor("pB", [128, G, TB + 1], F32)
        pF = ar2.sbuf_tensor("pF", [128, G, TB], F32)
        pFV = ar2.sbuf_tensor("pFV", [128, G, TB], F32)
        pR = ar2.sbuf_tensor("pR", [128, G, TB], BF16)
        pT1 = ar2.sbuf_tensor("pT1", [128, G, TB], F32)
        pT2 = ar2.sbuf_tensor("pT2", [128, G, TB], F32)
        rwkv = ar2.sbuf_tensor("rwkv", [128, HI, TB], BF16)
        wo = [ar2.sbuf_tensor(f"wo{i}", [128, 4 * 128], BF16) for i in range(2)]
        xr = ar2.sbuf_tensor("xr", [128, NTT, H], F32)
        l2buf = ar2.sbuf_tensor("l2buf", [128, NTT, H], F32)
        xlb2 = [ar2.sbuf_tensor(f"xlb2{i}", [128, H], BF16) for i in range(2)]
        l2o = [ar2.sbuf_tensor(f"l2o{i}", [128, H], F32) for i in range(2)]
        assert ar2.top <= LIMIT, ar2.top

        ar3 = Arena(ARENA_BASE)
        xlcm = ar3.sbuf_tensor("xlcm", [128, HI, FFN_TQ], BF16)
        hcm = ar3.sbuf_tensor("hcm", [128, OB, FFN_TQ], BF16)
        wgb = [ar3.sbuf_tensor(f"wgb{i}", [128, HI * 128], BF16) for i in range(2)]
        wub = [ar3.sbuf_tensor(f"wub{i}", [128, HI * 128], BF16) for i in range(2)]
        wdb = [ar3.sbuf_tensor(f"wdb{i}", [128, 4 * 128], BF16) for i in range(2)]
        gate_s = ar3.sbuf_tensor("gate_s", [128, TB], BF16)
        l3buf = ar3.sbuf_tensor("l3buf", [128, NTT, H], F32)
        xlT = [ar3.sbuf_tensor(f"xlT{i}", [128, TB], F32) for i in range(2)]
        xlb3 = [ar3.sbuf_tensor(f"xlb3{i}", [128, H], BF16) for i in range(2)]
        yt = [ar3.sbuf_tensor(f"yt{i}", [128, H], F32) for i in range(2)]
        sc3 = yt[0]
        assert ar3.top <= LIMIT, ar3.top

        psum = [ctx.enter_context(nc.psum_tensor(f"ps{i}", [128, 512], F32))
                for i in range(8)]
        block = ctx.enter_context(nc.Block())

        p.ev("sync", lambda e: e.dma_start(out=vec_s[:].rearrange("p (a c) -> p a c", a=7), in_=AP(vecs.tensor, 0, [[HI, 128], [128 * HI, 7], [1, HI]])), inc="s_x", n=16)
        p.ev("sync", lambda e: e.dma_start(out=lutb[:], in_=lutb_in), inc="s_x", n=16)
        p.ev("sync", lambda e: e.dma_start(out=maskv[:], in_=maskv_in), inc="s_x", n=16)
        p.ev("scalar", lambda e: e.activation(out=tmp16[:], in_=td, func=AF.Exp), waits=[("s_x", None)], inc="s_ac")
        p.ev("vector", lambda e: e.tensor_scalar_mul(out=wdec[:], in0=tmp16[:], scalar1=-1.0), waits=[("s_ac", None)], inc="s_ve")
        p.ev("scalar", lambda e: e.activation(out=ddec[:], in_=wdec[:], func=AF.Exp), waits=[("s_ve", None)], inc="s_ac")
        p.ev("scalar", lambda e: e.activation(out=dec512[:], in_=wdec[:], func=AF.Exp, scale=float(TB)), inc="s_ac")
        p.ev("vector", lambda e: e.tensor_scalar(out=onemr[:], in0=mrv, scalar1=-1.0, scalar2=1.0, op0=ALU.mult, op1=ALU.add), waits=[("s_x", None)], inc="s_ve")
        for buf in (ainit, binit, ain, bin_):
            p.ev("vector", lambda e, buf=buf: e.memset(buf[:], 0.0), inc="s_ve")
        p.ev("vector", lambda e: e.memset(epsv[:], LN_EPS), inc="s_ve")

        for m in range(4):
            for rt in range(2):
                p.ev("sync", lambda e, m=m, rt=rt: e.dma_start(out=awt[:], in_=wsl[m, rt * 128:(rt + 1) * 128, :]),
                     waits=[("s_gp", None)], inc="s_x", n=16)
                p.ev("vector", lambda e: e.tensor_copy(out=awb[:], in_=awt[:]), waits=[("s_x", None)], inc="s_ve")
                off = m * 16 * BLK2 + rt * 128
                p.ev("gpsimd", lambda e, off=off: e.dma_start(
                    out=AP(cc_w_in, off, [[256, 128], [BLK2, 16], [1, 128]]),
                    in_=awb[:].rearrange("p (t c) -> p t c", t=16)),
                    waits=[("s_ve", None)], inc="s_gp", n=16)

        def therm_stripe(cast_src_fn, acc_base, acc_ob_stride):
            p.ev("gpsimd", lambda e, f=cast_src_fn: e.dma_start(
                out=idxs[:].rearrange("p (b c) -> p b c", b=OB), in_=f()),
                waits=[("s_tr", None), ("s_gp", None)], inc="s_gp", n=16)
            for ob in range(OB):
                p.ev("sync", lambda e, ob=ob: e.dma_start(
                    out=idxT[:, ob * 128:(ob + 1) * 128],
                    in_=idxs[:, ob * 128:(ob + 1) * 128], transpose=True),
                    waits=[("s_gp", None), ("s_ve", None)], inc="s_tr", n=16)
            for j in range(16):
                wb = wterm[j % 2]
                p.ev("vector", lambda e, j=j, wb=wb: e.tensor_scalar(
                    out=wb[:], in0=idxT[:], scalar1=float(j), scalar2=lutb[:, j:j + 1],
                    op0=ALU.is_equal, op1=ALU.mult),
                    waits=[("s_tr", None), ("s_gp", None)], inc="s_ve")
                kw = {} if j == 0 else {"accum_op": ALU.add}
                p.ev("gpsimd", lambda e, wb=wb, kw=kw, base=acc_base, stp=acc_ob_stride: e.dma_start(
                    out=AP(cc_w_in, base, [[256, 128], [stp, OB], [1, 128]]),
                    in_=wb[:].rearrange("p (b c) -> p b c", b=OB), **kw),
                    waits=[("s_ve", None)], inc="s_gp", n=16)

        for (mat_t, mbase) in ((gidx.tensor, G_BASE), (uidx.tensor, U_BASE)):
            for ht in range(2):
                therm_stripe(
                    lambda mat_t=mat_t, ht=ht: AP(mat_t, ht * 128,
                                                  [[256, 128], [128 * 256, OB], [1, 128]]),
                    mbase + ht * 128, BLK2)
        for dt2 in range(2):
            therm_stripe(
                lambda dt2=dt2: AP(didx.tensor, dt2 * 128 * FFN_O,
                                   [[FFN_O, 128], [128, OB], [1, 128]]),
                D_BASE + dt2 * 128, BLK2)

        if dbg:
            p.ev("gpsimd", lambda e: e.dma_start(out=spWI.ap(), in_=cc_w_in.ap()),
                 waits=[("s_gp", None)], inc="s_gp", n=16)
        p.ev("gpsimd", lambda e: e.collective_compute(
            "AllGather", ALU.bypass, replica_groups=[list(range(NCORE))],
            ins=[cc_w_in.ap().opt()], outs=[cc_w_out.ap().opt()]),
            waits=[("s_gp", None)], inc="s_cc")
        if dbg:
            p.ev("gpsimd", lambda e: e.dma_start(
                out=spWO.ap(), in_=AP(cc_w_out, 0, [[1, 4 * RANK_SZ]])),
                waits=[("s_cc", None)], inc="s_gp", n=16)

        p.barrier()

        p.ev("sync", lambda e: e.dma_start(out=grep[:], in_=grep_in), inc="s_x", n=16)
        p.ev("sync", lambda e: e.dma_start(out=brep[:], in_=brep_in), inc="s_x", n=16)

        ln1_ctr = [0]
        pend = []

        def ln1_stats(rows0, bp, par, nr=128):
            p.ev("sync", lambda e, rows0=rows0, bp=bp, nr=nr: e.dma_start(
                out=xt[bp][:nr, :], in_=xh[rows0:rows0 + nr, :]),
                waits=[("s_ve", None)], inc="s_xr", n=16)
            p.ev("vector", lambda e, bp=bp, par=par, nr=nr: e.tensor_scalar(
                out=st1[:nr, :], in0=xt[bp][:nr, :], scalar1=1.0, scalar2=0.0,
                op0=ALU.mult, op1=ALU.add, accum_out=sx[:nr, par, 0:1]),
                waits=[("s_xr", None)], inc="s_ve")
            p.ev("vector", lambda e, bp=bp, par=par, nr=nr: e.scalar_tensor_tensor(
                out=st1[:nr, :], in0=xt[bp][:nr, :], scalar=1.0, in1=xt[bp][:nr, :],
                op0=ALU.mult, op1=ALU.mult, accum_out=sx[:nr, par, 1:2]), inc="s_ve")

        def ln1_apply(bp, par, tr_fn, nr=128, ti=None):
            p.ev("scalar", lambda e, par=par, nr=nr: e.activation(
                out=sy[:nr, par, 0:1], in_=sx[:nr, par, 0:1], func=AF.Identity,
                scale=1.0 / H), waits=[("s_ve", None)], inc="s_ac")
            p.ev("scalar", lambda e, par=par, nr=nr: e.activation(
                out=sx[:nr, par, 2:3], in_=sx[:nr, par, 0:1], func=AF.Square,
                scale=1.0 / H), inc="s_ac")
            p.ev("gpsimd", lambda e, par=par, nr=nr: e.tensor_scalar(
                out=sy[:nr, par, 2:3], in0=sx[:nr, par, 1:2], scalar1=1.0 / H,
                scalar2=sx[:nr, par, 2:3], op0=ALU.mult, op1=ALU.subtract),
                waits=[("s_ac", None), ("s_ve", None)], inc="s_gp")
            p.ev("scalar", lambda e, par=par, nr=nr: e.activation(
                out=sy[:nr, par, 2:3], in_=sy[:nr, par, 2:3], func=AF.Ln, bias=epsv[:nr, :]),
                waits=[("s_gp", None)], inc="s_ac")
            p.ev("scalar", lambda e, par=par, nr=nr: e.activation(
                out=sy[:nr, par, 2:3], in_=sy[:nr, par, 2:3], func=AF.Exp, scale=-0.5), inc="s_ac")
            if dbg and ti is not None and ti < 5:
                p.ev("vector", lambda e, ti=ti, par=par, nr=nr: e.tensor_copy(
                    out=stlog[:nr, 4 * ti + 3:4 * ti + 4], in_=sy[:nr, par, 2:3]),
                    waits=[("s_ac", None)], inc="s_ve")
            p.ev("vector", lambda e, bp=bp, par=par, nr=nr: e.tensor_scalar(
                out=st1[:nr, :], in0=xt[bp][:nr, :], scalar1=sy[:nr, par, 0:1], scalar2=sy[:nr, par, 2:3],
                op0=ALU.subtract, op1=ALU.mult), waits=[("s_ac", None)], inc="s_ve")
            p.ev("vector", lambda e, bp=bp, nr=nr: e.tensor_tensor(
                out=xnb[bp][:nr, :], in0=st1[:nr, :], in1=grep[:nr, :], op=ALU.mult),
                waits=[("s_tr", None)], inc="s_ve")
            p.ev("vector", lambda e, bp=bp, nr=nr: e.tensor_tensor(
                out=xnb[bp][:nr, :], in0=xnb[bp][:nr, :], in1=brep[:nr, :], op=ALU.add), inc="s_ve")
            tr_fn(bp)

        def ln1_tile(rows0, bp, tr_fn, nr=128):
            ti = ln1_ctr[0]; ln1_ctr[0] += 1
            par = ti % 4
            ln1_stats(rows0, par, par, nr)
            if len(pend) >= 2:
                ln1_apply(*pend.pop(0))
            pend.append((par, par, tr_fn, nr, ti))

        def ln1_flush():
            p.ev("vector", lambda e: e.drain(), inc="s_ve")
            p.ev("vector", lambda e: e.nop(cycle_cnt=2000), inc=None) if False else None
            p.ev("vector", lambda e: e.drain(), inc="s_ve")
            while pend:
                ln1_apply(*pend.pop(0))

        def tr_to(buf, c0):
            def f(bp):
                for hi in range(HI):
                    p.ev("sync", lambda e, bp=bp, hi=hi, buf=buf, c0=c0: e.dma_start(
                        out=buf[:, hi, c0:c0 + 128],
                        in_=xnb[bp][:, hi * 128:(hi + 1) * 128], transpose=True),
                        waits=[("s_ve", None)], inc="s_tr", n=16)
            return f

        def tr_to_scratch(bp):
            for hi in range(HI):
                p.ev("sync", lambda e, bp=bp, hi=hi: e.dma_start(
                    out=xsc[:, hi, :],
                    in_=xnb[bp][:, hi * 128:(hi + 1) * 128], transpose=True),
                    waits=[("s_ve", None)], inc="s_tr", n=16)

        bank_free = [None] * 8

        def mm_group(bank, pairs, extra_waits=()):
            w = list(extra_waits)
            if bank_free[bank] is not None:
                w.append(bank_free[bank])
            n = len(pairs)
            for i, (lf, rf) in enumerate(pairs):
                p.ev("tensor", lambda e, lf=lf, rf=rf, i=i, n=n, bank=bank: e.matmul(
                    psum[bank][:], lf(), rf(), start=(i == 0), stop=(i == n - 1)),
                    waits=(w if i == 0 else ()), inc=("s_mm" if i == n - 1 else None))
            return ("s_mm", p.cnt["s_mm"])

        def wa_load(mat, dt, par):
            off = mat * 16 * BLK2 + dt * BLK2
            p.ev("sync", lambda e, off=off, par=par: e.dma_start(
                out=wa[par][:],
                in_=AP(cc_w_out, off, [[256, 128], [RANK_SZ, 8], [1, 256]])),
                waits=[("s_cc", None), ("s_mm", None)], inc="s_x", n=16)

        def ln1_block(b):
            buf = xncm[b % 2]
            for t in range(NTT):
                ln1_tile(b * TB + t * 128, t % 2, tr_to(buf, t * 128))
            if b == NBLK - 1:
                ln1_tile(TPC - 127, 0, tr_to_scratch)
                ln1_flush()

        def compute_block(b):
            buf = xncm[b % 2]
            if b < NBLK - 1:
                nbuf = xncm[(b + 1) % 2]
                p.ev("vector", lambda e, buf=buf, nbuf=nbuf: e.tensor_copy(
                    out=buf[:, :, TB], in_=nbuf[:, :, 0]),
                    waits=[("s_tr", None)], inc="s_ve")
            else:
                p.ev("vector", lambda e, buf=buf: e.tensor_copy(
                    out=buf[:, :, TB], in_=xsc[:, :, 127]),
                    waits=[("s_tr", None)], inc="s_ve")
            if b == 0:
                p.ev("gpsimd", lambda e, buf=buf: e.tensor_copy(
                    out=tmp16[:], in_=buf[:, :, 0]),
                    waits=[("s_tr", None)], inc="s_gp")
                p.ev("gpsimd", lambda e, buf=buf: e.tensor_scalar_mul(
                    out=buf[:, :, 0], in0=tmp16[:], scalar1=maskv[:]), inc="s_gp")
            x1 = buf[:, :, 1:TB + 1]
            x0 = buf[:, :, 0:TB]
            p.ev("vector", lambda e, x1=x1, x0=x0: e.tensor_tensor(
                out=vx[:], in0=x1, in1=x0, op=ALU.subtract),
                waits=[("s_tr", None), ("s_mm", None), ("s_gp", None)], inc="s_ve")
            for hi in range(HI):
                p.ev("vector", lambda e, hi=hi, x0=x0: e.scalar_tensor_tensor(
                    out=kx[:, hi, :], in0=vx[:, hi, :], scalar=mkv[:, hi:hi + 1],
                    in1=x0[:, hi, :], op0=ALU.mult, op1=ALU.add), inc="s_ve")
            for hi in range(HI):
                p.ev("vector", lambda e, hi=hi, x0=x0: e.scalar_tensor_tensor(
                    out=vx[:, hi, :], in0=vx[:, hi, :], scalar=mvv[:, hi:hi + 1],
                    in1=x0[:, hi, :], op0=ALU.mult, op1=ALU.add), inc="s_ve")
            mix_done = ("s_ve", p.cnt["s_ve"])
            if dbg and b < 2:
                p.ev("sync", lambda e, b=b, buf=buf: e.dma_start(
                    out=AP(spXN, b * 128 * HI * TBP,
                           [[HI * TBP, 128], [TBP, HI], [1, TBP]]),
                    in_=buf[:]), waits=[("s_ve", None)], inc="s_sp", n=16)
            if dbg and b == 0:
                p.ev("sync", lambda e: e.dma_start(
                    out=AP(spKX, 0, [[HI * TB, 128], [TB, HI], [1, TB]]),
                    in_=kx[:]), waits=[("s_ve", None)], inc="s_sp", n=16)

            for dt in range(HI):
                par = dt % 2
                gp = (dt // G) % 2
                j = dt % G
                wa_load(0, dt, par)
                fb = mm_group(dt % 2, [
                    (lambda kt=kt, par=par: wa[par][:, kt * 128:(kt + 1) * 128],
                     lambda kt=kt: kx[:, kt, :]) for kt in range(HI)],
                    extra_waits=[("s_x", None), mix_done])
                p.ev("scalar", lambda e, gp=gp, j=j, bk=dt % 2: e.copy(
                    out=kg[gp][:, j, :], in_=psum[bk][:]),
                    waits=[fb, ("s_ve", None), ("s_sp", None)], inc="s_ac")
                bank_free[dt % 2] = ("s_ac", p.cnt["s_ac"])
                if dbg:
                    p.ev("sync", lambda e, gp=gp, j=j, b=b, dt=dt: e.dma_start(
                        out=AP(spK, ((b * HI + dt) * 128) * TB, [[TB, 128], [1, TB]]),
                        in_=kg[gp][:, j, :]), waits=[("s_ac", None)], inc="s_sp", n=16)
                wa_load(1, dt, par)
                fb = mm_group(2 + dt % 2, [
                    (lambda kt=kt, par=par: wa[par][:, kt * 128:(kt + 1) * 128],
                     lambda kt=kt: vx[:, kt, :]) for kt in range(HI)],
                    extra_waits=[("s_x", None)])
                p.ev("scalar", lambda e, gp=gp, j=j, bk=2 + dt % 2: e.copy(
                    out=vg[gp][:, j, :], in_=psum[bk][:]),
                    waits=[fb, ("s_ve", None), ("s_sp", None)], inc="s_ac")
                bank_free[2 + dt % 2] = ("s_ac", p.cnt["s_ac"])
                if dbg:
                    p.ev("sync", lambda e, gp=gp, j=j, b=b, dt=dt: e.dma_start(
                        out=AP(spV, ((b * HI + dt) * 128) * TB, [[TB, 128], [1, TB]]),
                        in_=vg[gp][:, j, :]), waits=[("s_ac", None)], inc="s_sp", n=16)

                if j == G - 1:
                    g0 = dt - G + 1
                    bhi0 = b * HI + g0
                    for j2 in range(G):
                        p.ev("vector", lambda e, gp=gp, j2=j2, bhi0=bhi0: e.reduce_max(
                            out=pneg[:, bhi0 + j2:bhi0 + j2 + 1], in_=kg[gp][:, j2, :],
                            axis=AXX, negate=True), waits=[("s_ac", None)], inc="s_ve")
                    p.ev("gpsimd", lambda e, g0=g0, bhi0=bhi0: e.tensor_tensor(
                        out=ubias[:, bhi0:bhi0 + G], in0=uu[:, g0:g0 + G],
                        in1=pneg[:, bhi0:bhi0 + G], op=ALU.add),
                        waits=[("s_ve", None)], inc="s_gp")
                    if b > 0:
                        pbhi0 = (b - 1) * HI + g0
                        p.ev("gpsimd", lambda e, bhi0=bhi0, pbhi0=pbhi0, g0=g0: e.tensor_tensor(
                            out=tmp16[:, g0:g0 + G], in0=pneg[:, bhi0:bhi0 + G],
                            in1=pneg[:, pbhi0:pbhi0 + G], op=ALU.subtract), inc="s_gp")
                        p.ev("scalar", lambda e, g0=g0: e.activation(
                            out=tmp16[:, g0:g0 + G], in_=tmp16[:, g0:g0 + G], func=AF.Exp),
                            waits=[("s_gp", None)], inc="s_ac")
                        p.ev("gpsimd", lambda e, g0=g0: e.tensor_tensor(
                            out=ainit[:, g0:g0 + G], in0=aend[:, g0:g0 + G],
                            in1=tmp16[:, g0:g0 + G], op=ALU.mult),
                            waits=[("s_ac", None)], inc="s_gp")
                        p.ev("gpsimd", lambda e, g0=g0: e.tensor_tensor(
                            out=binit[:, g0:g0 + G], in0=bend[:, g0:g0 + G],
                            in1=tmp16[:, g0:g0 + G], op=ALU.mult), inc="s_gp")
                    ve_stats = p.cnt["s_ve"]
                    for j2 in range(G):
                        p.ev("scalar", lambda e, gp=gp, j2=j2, bhi0=bhi0: e.activation(
                            out=Ee[:, j2, :], in_=kg[gp][:, j2, :], func=AF.Exp,
                            bias=pneg[:, bhi0 + j2:bhi0 + j2 + 1]),
                            waits=[("s_ve", ve_stats), ("s_sp", None)], inc="s_ac")
                        p.ev("scalar", lambda e, gp=gp, j2=j2, bhi0=bhi0: e.activation(
                            out=Ff[:, j2, :], in_=kg[gp][:, j2, :], func=AF.Exp,
                            bias=ubias[:, bhi0 + j2:bhi0 + j2 + 1]), inc="s_ac")
                    for j2 in range(G):
                        hi = g0 + j2
                        p.ev("vector", lambda e, j2=j2, hi=hi: e.tensor_copy(
                            out=scB[:, j2, 0:1], in_=binit[:, hi:hi + 1]),
                            waits=[("s_ac", None), ("s_sp", None), ("s_gp", None)], inc="s_ve")
                        p.ev("vector", lambda e, j2=j2, hi=hi: e.tensor_tensor_scan(
                            out=scB[:, j2, 1:TB + 1],
                            data0=ddec[:, hi:hi + 1].broadcast_to([128, TB]),
                            data1=Ee[:, j2, :], initial=binit[:, hi:hi + 1],
                            op0=ALU.mult, op1=ALU.add), inc="s_ve")
                    p.ev("vector", lambda e, gp=gp: e.tensor_tensor(
                        out=Ee[:], in0=Ee[:], in1=vg[gp][:], op=ALU.mult), inc="s_ve")
                    p.ev("vector", lambda e, gp=gp: e.tensor_tensor(
                        out=FVv[:], in0=Ff[:], in1=vg[gp][:], op=ALU.mult), inc="s_ve")
                    for j2 in range(G):
                        hi = g0 + j2
                        p.ev("vector", lambda e, j2=j2, hi=hi: e.tensor_copy(
                            out=scA[:, j2, 0:1], in_=ainit[:, hi:hi + 1]), inc="s_ve")
                        p.ev("vector", lambda e, j2=j2, hi=hi: e.tensor_tensor_scan(
                            out=scA[:, j2, 1:TB + 1],
                            data0=ddec[:, hi:hi + 1].broadcast_to([128, TB]),
                            data1=Ee[:, j2, :], initial=ainit[:, hi:hi + 1],
                            op0=ALU.mult, op1=ALU.add), inc="s_ve")
                    p.ev("gpsimd", lambda e, g0=g0: e.tensor_copy(
                        out=aend[:, g0:g0 + G], in_=scA[:, :, TB]),
                        waits=[("s_ve", None)], inc="s_gp")
                    p.ev("gpsimd", lambda e, g0=g0: e.tensor_copy(
                        out=bend[:, g0:g0 + G], in_=scB[:, :, TB]), inc="s_gp")
                    ve_sc = p.cnt["s_ve"]
                    for (t_, sp, C) in ((scA, spA, TB + 1), (scB, spB, TB + 1),
                                        (Ff, spF, TB), (FVv, spFV, TB)):
                        p.ev("sync", lambda e, t_=t_, sp=sp, C=C, b=b, g0=g0: e.dma_start(
                            out=AP(sp, (b * HI + g0) * 128 * C, [[C, 128], [128 * C, G], [1, C]]),
                            in_=t_[:, :, 0:C] if C == TB else t_[:, :, :]),
                            waits=[("s_ve", ve_sc), ("s_ac", None)], inc="s_sp", n=16)

            kvr_done = ("s_mm", p.cnt["s_mm"])
            for hi in range(HI):
                p.ev("vector", lambda e, hi=hi, x1=x1: e.tensor_scalar_mul(
                    out=kx[:, hi, :], in0=x1[:, hi, :], scalar1=mrv[:, hi:hi + 1]),
                    waits=([kvr_done] if hi == 0 else ()), inc="s_ve")
            for hi in range(HI):
                p.ev("vector", lambda e, hi=hi, x0=x0: e.scalar_tensor_tensor(
                    out=kx[:, hi, :], in0=x0[:, hi, :], scalar=onemr[:, hi:hi + 1],
                    in1=kx[:, hi, :], op0=ALU.mult, op1=ALU.add), inc="s_ve")
            rx_done = ("s_ve", p.cnt["s_ve"])
            for dt in range(HI):
                par = dt % 2
                wa_load(2, dt, par)
                rbank = 4 + dt % 2
                fb = mm_group(rbank, [
                    (lambda kt=kt, par=par: wa[par][:, kt * 128:(kt + 1) * 128],
                     lambda kt=kt: kx[:, kt, :]) for kt in range(HI)],
                    extra_waits=[("s_x", None), rx_done])
                p.ev("scalar", lambda e, dt=dt, rbank=rbank: e.activation(
                    out=sigr[:, dt, :], in_=psum[rbank][:], func=AF.Sigmoid),
                    waits=[fb, ("s_sp", None)], inc="s_ac")
                bank_free[rbank] = ("s_ac", p.cnt["s_ac"])
            p.ev("sync", lambda e, b=b: e.dma_start(
                out=AP(spR, b * HI * 128 * TB, [[TB, 128], [128 * TB, HI], [1, TB]]),
                in_=sigr[:]), waits=[("s_ac", None)], inc="s_sp", n=16)

        for b in range(NBLK + 1):
            if b < NBLK:
                ln1_block(b)
            if b >= 1:
                compute_block(b - 1)

        p.barrier()

        p.ev("sync", lambda e: e.dma_start(out=ramp0b[:], in_=ramp0_in), inc="s_x", n=16)
        for hi in range(HI):
            p.ev("vector", lambda e, hi=hi: e.tensor_scalar_mul(
                out=Dsh[:, hi, :], in0=ramp0b[:], scalar1=wdec[:, hi:hi + 1]),
                waits=[("s_x", None)], inc="s_ve")
        p.ev("scalar", lambda e: e.activation(out=Dsh[:], in_=Dsh[:], func=AF.Exp),
             waits=[("s_ve", None)], inc="s_ac")

        for b in range(NBLK):
            if b > 0:
                p.ev("gpsimd", lambda e, b=b: e.tensor_tensor(
                    out=tmp16[:], in0=pneg[:, b * HI:(b + 1) * HI],
                    in1=pneg[:, (b - 1) * HI:b * HI], op=ALU.subtract),
                    waits=[("s_ac", None), ("s_ve", None)], inc="s_gp")
                p.ev("scalar", lambda e: e.activation(out=tmp16[:], in_=tmp16[:], func=AF.Exp),
                     waits=[("s_gp", None)], inc="s_ac")
                p.ev("gpsimd", lambda e: e.tensor_tensor(out=tmp16b[:], in0=ain[:], in1=dec512[:], op=ALU.mult),
                     waits=[("s_ac", None)], inc="s_gp")
                p.ev("gpsimd", lambda e: e.tensor_tensor(out=ain[:], in0=tmp16b[:], in1=tmp16[:], op=ALU.mult), inc="s_gp")
                p.ev("gpsimd", lambda e: e.tensor_tensor(out=tmp16b[:], in0=bin_[:], in1=dec512[:], op=ALU.mult), inc="s_gp")
                p.ev("gpsimd", lambda e: e.tensor_tensor(out=bin_[:], in0=tmp16b[:], in1=tmp16[:], op=ALU.mult), inc="s_gp")
            for g in range(NG):
                g0 = g * G
                ve_used = p.cnt["s_ve"]
                for (t_, sp, C) in ((pA, spA, TB + 1), (pB, spB, TB + 1),
                                    (pF, spF, TB), (pFV, spFV, TB)):
                    p.ev("sync", lambda e, t_=t_, sp=sp, C=C, b=b, g0=g0: e.dma_start(
                        out=t_[:, :, 0:C] if C == TB else t_[:, :, :],
                        in_=AP(sp, (b * HI + g0) * 128 * C, [[C, 128], [128 * C, G], [1, C]])),
                        waits=[("s_ve", ve_used)], inc="s_x", n=16)
                p.ev("sync", lambda e, b=b, g0=g0: e.dma_start(
                    out=pR[:],
                    in_=AP(spR, (b * HI + g0) * 128 * TB, [[TB, 128], [128 * TB, G], [1, TB]])),
                    waits=[("s_ve", ve_used)], inc="s_x", n=16)
                xw = [("s_x", None), ("s_ac", None), ("s_gp", None)]
                for j2 in range(G):
                    hi = g0 + j2
                    p.ev("vector", lambda e, j2=j2, hi=hi: e.scalar_tensor_tensor(
                        out=pT1[:, j2, :], in0=Dsh[:, hi, :], scalar=ain[:, hi:hi + 1],
                        in1=pA[:, j2, 0:TB], op0=ALU.mult, op1=ALU.add),
                        waits=(xw if j2 == 0 else ()), inc="s_ve")
                    p.ev("vector", lambda e, j2=j2, hi=hi: e.scalar_tensor_tensor(
                        out=pT2[:, j2, :], in0=Dsh[:, hi, :], scalar=bin_[:, hi:hi + 1],
                        in1=pB[:, j2, 0:TB], op0=ALU.mult, op1=ALU.add), inc="s_ve")
                p.ev("vector", lambda e: e.tensor_tensor(out=pT1[:], in0=pT1[:], in1=pFV[:], op=ALU.add), inc="s_ve")
                p.ev("vector", lambda e: e.tensor_tensor(out=pT2[:], in0=pT2[:], in1=pF[:], op=ALU.add), inc="s_ve")
                p.ev("scalar", lambda e: e.activation(out=pT2[:], in_=pT2[:], func=AF.Ln),
                     waits=[("s_ve", None)], inc="s_ac")
                p.ev("scalar", lambda e: e.activation(out=pT2[:], in_=pT2[:], func=AF.Exp, scale=-1.0), inc="s_ac")
                p.ev("vector", lambda e: e.tensor_tensor(out=pT1[:], in0=pT1[:], in1=pT2[:], op=ALU.mult),
                     waits=[("s_ac", None)], inc="s_ve")
                if dbg:
                    p.ev("sync", lambda e, b=b, g0=g0: e.dma_start(
                        out=AP(spWKV, (b * HI + g0) * 128 * TB, [[TB, 128], [128 * TB, G], [1, TB]]),
                        in_=pT1[:]), waits=[("s_ve", None)], inc="s_sp", n=16)
                p.ev("vector", lambda e, g0=g0: e.tensor_tensor(
                    out=rwkv[:, g0:g0 + G, :], in0=pT1[:], in1=pR[:], op=ALU.mult), inc="s_ve")
            rw_done = ("s_ve", p.cnt["s_ve"])
            if dbg:
                p.ev("sync", lambda e, b=b: e.dma_start(
                    out=AP(spRW, b * 128 * HI * TB, [[HI * TB, 128], [TB, HI], [1, TB]]),
                    in_=rwkv[:]), waits=[rw_done], inc="s_sp", n=16)

            for tt in range(NTT):
                p.ev("sync", lambda e, b=b, tt=tt: e.dma_start(
                    out=xr[:, tt, :], in_=xh[1 + b * TB + tt * 128: 1 + b * TB + (tt + 1) * 128, :]),
                    waits=[("s_ve", None)], inc="s_xr", n=16)

            for q in range(4):
                for kt in range(HI):
                    parw = kt % 2
                    off = (kt // 2) * RANK_SZ + 3 * 16 * BLK2 + q * 4 * BLK2 + (kt % 2) * 128
                    p.ev("sync", lambda e, off=off, parw=parw: e.dma_start(
                        out=wo[parw][:].rearrange("p (t c) -> p t c", t=4),
                        in_=AP(cc_w_out, off, [[256, 128], [BLK2, 4], [1, 128]])),
                        waits=[("s_mm", None)], inc="s_x", n=16)
                    for tt in range(4):
                        w = []
                        if tt == 0:
                            w = [("s_x", None)]
                        if kt == 0:
                            w = [("s_x", None), rw_done]
                            if bank_free[4 + tt] is not None:
                                w.append(bank_free[4 + tt])
                        p.ev("tensor", lambda e, kt=kt, tt=tt, parw=parw: e.matmul(
                            psum[4 + tt][:], rwkv[:, kt, tt * 128:(tt + 1) * 128],
                            wo[parw][:], start=(kt == 0), stop=(kt == HI - 1)),
                            waits=w, inc=("s_mm" if tt == 3 else None))
                mmv = p.cnt["s_mm"]
                for tt in range(4):
                    p.ev("vector", lambda e, tt=tt, q=q: e.scalar_tensor_tensor(
                        out=l2buf[:, tt, q * 512:(q + 1) * 512], in0=xr[:, tt, q * 512:(q + 1) * 512],
                        scalar=2.0, in1=psum[4 + tt][:], op0=ALU.mult, op1=ALU.add),
                        waits=[("s_mm", mmv), ("s_xr", None), ("s_sp", None)], inc="s_ve")
                p.ev("vector", lambda e: e.nop(), inc="s_ve")
                p.ev("vector", lambda e: e.nop(), inc="s_ve")
                for tt in range(4):
                    bank_free[4 + tt] = ("s_ve", p.cnt["s_ve"])

            if dbg:
                p.ev("sync", lambda e, b=b: e.dma_start(
                    out=AP(spL2, b * 128 * NTT * H, [[NTT * H, 128], [H, NTT], [1, H]]),
                    in_=l2buf[:]), waits=[("s_ve", None)], inc="s_sp", n=16)
            for tt in range(NTT):
                p.ev("vector", lambda e, tt=tt: e.tensor_scalar(
                    out=xr[:, 0, :], in0=l2buf[:, tt, :], scalar1=1.0,
                    scalar2=0.0, op0=ALU.mult, op1=ALU.add, accum_out=lnsx[:, tt, 0:1]), inc="s_ve")
                p.ev("vector", lambda e, tt=tt: e.scalar_tensor_tensor(
                    out=xr[:, 0, :],
                    in0=l2buf[:, tt, :], scalar=1.0, in1=l2buf[:, tt, :],
                    op0=ALU.mult, op1=ALU.mult, accum_out=lnsx[:, tt, 1:2]), inc="s_ve")
            for tt in range(NTT):
                p.ev("scalar", lambda e, tt=tt: e.activation(
                    out=lnsy[:, tt, 0:1], in_=lnsx[:, tt, 0:1], func=AF.Identity,
                    scale=1.0 / H), waits=[("s_ve", None)], inc="s_ac")
                p.ev("scalar", lambda e, tt=tt: e.activation(
                    out=lnsx[:, tt, 2:3], in_=lnsx[:, tt, 0:1], func=AF.Square,
                    scale=1.0 / H), inc="s_ac")
                p.ev("gpsimd", lambda e, tt=tt: e.tensor_scalar(
                    out=lnsy[:, tt, 2:3], in0=lnsx[:, tt, 1:2], scalar1=1.0 / H,
                    scalar2=lnsx[:, tt, 2:3], op0=ALU.mult, op1=ALU.subtract),
                    waits=[("s_ac", None), ("s_ve", None)], inc="s_gp")
                p.ev("scalar", lambda e, tt=tt: e.activation(out=lnsy[:, tt, 2:3], in_=lnsy[:, tt, 2:3], func=AF.Ln, bias=epsv[:]), waits=[("s_gp", None)], inc="s_ac")
                p.ev("scalar", lambda e, tt=tt: e.activation(out=lnsy[:, tt, 2:3], in_=lnsy[:, tt, 2:3], func=AF.Exp, scale=-0.5), inc="s_ac")
                p.ev("vector", lambda e, tt=tt: e.tensor_scalar(
                    out=l2o[tt % 2][:], in0=l2buf[:, tt, :], scalar1=lnsy[:, tt, 0:1], scalar2=lnsy[:, tt, 2:3],
                    op0=ALU.subtract, op1=ALU.mult),
                    waits=[("s_ac", None), ("s_sp", None)], inc="s_ve")
                p.ev("vector", lambda e, tt=tt: e.tensor_copy(
                    out=xlb2[tt % 2][:], in_=l2o[tt % 2][:]),
                    waits=[("s_sp", None)], inc="s_ve")
                rows0 = b * TB + tt * 128
                p.ev("sync", lambda e, tt=tt, rows0=rows0: e.dma_start(
                    out=AP(spX, rows0 * H, [[H, 128], [1, H]]), in_=l2o[tt % 2][:]),
                    waits=[("s_ve", None)], inc="s_sp", n=16)
                p.ev("sync", lambda e, tt=tt, rows0=rows0: e.dma_start(
                    out=AP(spXB, rows0 * H, [[H, 128], [1, H]]), in_=xlb2[tt % 2][:]),
                    waits=[("s_ve", None)], inc="s_sp", n=16)

        p.barrier()

        for hh in range(NHALF):
            for i in range(FFN_TQ // 128):
                rows0 = hh * FFN_TQ + i * 128
                bp = i % 2
                p.ev("sync", lambda e, rows0=rows0, bp=bp: e.dma_start(
                    out=xlb3[bp][:], in_=AP(spXB, rows0 * H, [[H, 128], [1, H]])),
                    waits=[("s_tr", None), ("s_sp", None), ("s_mm", None)], inc="s_x", n=16)
                for hi in range(HI):
                    p.ev("sync", lambda e, bp=bp, hi=hi, i=i: e.dma_start(
                        out=xlcm[:, hi, i * 128:(i + 1) * 128],
                        in_=xlb3[bp][:, hi * 128:(hi + 1) * 128], transpose=True),
                        waits=[("s_x", None)], inc="s_tr", n=16)
            tr_fill = ("s_tr", p.cnt["s_tr"])

            for ot in range(OB):
                par = ot % 2
                p.ev("sync", lambda e, ot=ot, par=par: e.dma_start(
                    out=wgb[par][:],
                    in_=AP(cc_w_out, G_BASE + ot * BLK2,
                           [[256, 128], [RANK_SZ, 8], [1, 256]])),
                    waits=[("s_mm", None)], inc="s_x", n=16)
                p.ev("sync", lambda e, ot=ot, par=par: e.dma_start(
                    out=wub[par][:],
                    in_=AP(cc_w_out, U_BASE + ot * BLK2,
                           [[256, 128], [RANK_SZ, 8], [1, 256]])),
                    inc="s_x", n=16)
                for tq in range(FFN_TQ // TB):
                    gb = tq
                    ub = 2 + tq
                    fbg = mm_group(gb, [
                        (lambda kt=kt, par=par: wgb[par][:, kt * 128:(kt + 1) * 128],
                         lambda kt=kt, tq=tq: xlcm[:, kt, tq * TB:(tq + 1) * TB]) for kt in range(HI)],
                        extra_waits=[("s_x", None), ("s_cc", None), tr_fill])
                    p.ev("scalar", lambda e, gb=gb: e.activation(
                        out=gate_s[:], in_=psum[gb][:], func=AF.Silu),
                        waits=[fbg, ("s_ve", None)], inc="s_ac")
                    bank_free[gb] = ("s_ac", p.cnt["s_ac"])
                    fbu = mm_group(ub, [
                        (lambda kt=kt, par=par: wub[par][:, kt * 128:(kt + 1) * 128],
                         lambda kt=kt, tq=tq: xlcm[:, kt, tq * TB:(tq + 1) * TB]) for kt in range(HI)],
                        extra_waits=[("s_x", None)])
                    p.ev("vector", lambda e, ot=ot, tq=tq, ub=ub: e.tensor_tensor(
                        out=hcm[:, ot, tq * TB:(tq + 1) * TB], in0=psum[ub][:],
                        in1=gate_s[:], op=ALU.mult),
                        waits=[fbu, ("s_ac", None)], inc="s_ve")
                    p.ev("vector", lambda e: e.nop(), inc="s_ve")
                    bank_free[ub] = ("s_ve", p.cnt["s_ve"])
            h_done = ("s_ve", p.cnt["s_ve"])

            for ttg in range(2):
                for q in range(4):
                    for ob in range(OB):
                        parw = ob % 2
                        off = (2 * q) * RANK_SZ + D_BASE + ob * BLK2
                        p.ev("sync", lambda e, off=off, parw=parw: e.dma_start(
                            out=wdb[parw][:],
                            in_=AP(cc_w_out, off, [[256, 128], [RANK_SZ, 2], [1, 256]])),
                            waits=[("s_mm", None)], inc="s_x", n=16)
                        for tl in range(4):
                            tt = ttg * 4 + tl
                            w = []
                            if tl == 0:
                                w = [("s_x", None)]
                            if ob == 0:
                                w = [("s_x", None), h_done]
                                if bank_free[tl] is not None:
                                    w.append(bank_free[tl])
                            p.ev("tensor", lambda e, ob=ob, tl=tl, tt=tt, parw=parw: e.matmul(
                                psum[tl][:], hcm[:, ob, tt * 128:(tt + 1) * 128],
                                wdb[parw][:], start=(ob == 0), stop=(ob == OB - 1)),
                                waits=w, inc=("s_mm" if tl == 3 else None))
                    mmv = p.cnt["s_mm"]
                    for tl in range(4):
                        tt = ttg * 4 + tl
                        rows0 = hh * FFN_TQ + tt * 128
                        bp = tl % 2
                        p.ev("sync", lambda e, rows0=rows0, q=q, bp=bp: e.dma_start(
                            out=xlT[bp][:], in_=AP(spX, rows0 * H + q * 512, [[H, 128], [1, 512]])),
                            waits=[("s_ve", None)], inc="s_xr", n=16)
                        p.ev("vector", lambda e, tl=tl, q=q, bp=bp: e.scalar_tensor_tensor(
                            out=l3buf[:, tl, q * 512:(q + 1) * 512], in0=xlT[bp][:],
                            scalar=1.0, in1=psum[tl][:], op0=ALU.mult, op1=ALU.add),
                            waits=[("s_mm", mmv), ("s_xr", None), ("s_st", None)], inc="s_ve")
                    p.ev("vector", lambda e: e.nop(), inc="s_ve")
                    p.ev("vector", lambda e: e.nop(), inc="s_ve")
                    for tl in range(4):
                        bank_free[tl] = ("s_ve", p.cnt["s_ve"])
                for tl in range(4):
                    p.ev("vector", lambda e, tl=tl: e.tensor_scalar(
                        out=sc3[:], in0=l3buf[:, tl, :], scalar1=1.0,
                        scalar2=0.0, op0=ALU.mult, op1=ALU.add, accum_out=lnsx[:, tl, 0:1]), inc="s_ve")
                    p.ev("vector", lambda e, tl=tl: e.scalar_tensor_tensor(
                        out=sc3[:],
                        in0=l3buf[:, tl, :], scalar=1.0, in1=l3buf[:, tl, :],
                        op0=ALU.mult, op1=ALU.mult, accum_out=lnsx[:, tl, 1:2]), inc="s_ve")
                for tl in range(4):
                    tt = ttg * 4 + tl
                    p.ev("scalar", lambda e, tl=tl: e.activation(
                        out=lnsy[:, tl, 0:1], in_=lnsx[:, tl, 0:1], func=AF.Identity,
                        scale=1.0 / H), waits=[("s_ve", None)], inc="s_ac")
                    p.ev("scalar", lambda e, tl=tl: e.activation(
                        out=lnsx[:, tl, 2:3], in_=lnsx[:, tl, 0:1], func=AF.Square,
                        scale=1.0 / H), inc="s_ac")
                    p.ev("gpsimd", lambda e, tl=tl: e.tensor_scalar(
                        out=lnsy[:, tl, 2:3], in0=lnsx[:, tl, 1:2], scalar1=1.0 / H,
                        scalar2=lnsx[:, tl, 2:3], op0=ALU.mult, op1=ALU.subtract),
                        waits=[("s_ac", None), ("s_ve", None)], inc="s_gp")
                    p.ev("scalar", lambda e, tl=tl: e.activation(out=lnsy[:, tl, 2:3], in_=lnsy[:, tl, 2:3], func=AF.Ln, bias=epsv[:]), waits=[("s_gp", None)], inc="s_ac")
                    p.ev("scalar", lambda e, tl=tl: e.activation(out=lnsy[:, tl, 2:3], in_=lnsy[:, tl, 2:3], func=AF.Exp, scale=-0.5), inc="s_ac")
                    p.ev("vector", lambda e, tl=tl: e.tensor_scalar(
                        out=yt[tl % 2][:], in0=l3buf[:, tl, :], scalar1=lnsy[:, tl, 0:1], scalar2=lnsy[:, tl, 2:3],
                        op0=ALU.subtract, op1=ALU.mult),
                        waits=[("s_ac", None), ("s_st", None)], inc="s_ve")
                    rows0 = hh * FFN_TQ + tt * 128
                    p.ev("sync", lambda e, tl=tl, rows0=rows0: e.dma_start(
                        out=y[rows0:rows0 + 128, :], in_=yt[tl % 2][:]),
                        waits=[("s_ve", None)], inc="s_st", n=16)

        if dbg:
            p.ev("sync", lambda e: e.dma_start(out=spST.ap(), in_=stlog[:]),
                 waits=[("s_ve", None)], inc="s_sp", n=16)
        p.ev("gpsimd", None, waits=[("s_st", None), ("s_sp", None)])

        @block.sync
        def _(e):
            p.emit("sync", e, sems)

        @block.gpsimd
        def _(e):
            p.emit("gpsimd", e, sems)

        @block.vector
        def _(e):
            p.emit("vector", e, sems)

        @block.scalar
        def _(e):
            p.emit("scalar", e, sems)

        @block.tensor
        def _(e):
            p.emit("tensor", e, sems)

    return nc


_NC_CACHE = {}


def _get_nc(dbg=False):
    if dbg not in _NC_CACHE:
        _NC_CACHE[dbg] = build_program(dbg=dbg)
    return _NC_CACHE[dbg]


def make_in_maps(inputs):
    x = np.ascontiguousarray(np.asarray(inputs["L0_attn_x"], dtype=np.float32))
    Wk = np.asarray(inputs["L0_attn_Wk"], dtype=np.float32)
    Wv = np.asarray(inputs["L0_attn_Wv"], dtype=np.float32)
    Wr = np.asarray(inputs["L0_attn_Wr"], dtype=np.float32)
    Wo = np.asarray(inputs["L0_attn_Wo"], dtype=np.float32)
    scale = np.asarray(inputs["L0_attn_ln_scale"], dtype=np.float32)
    bias = np.asarray(inputs["L0_attn_ln_bias"], dtype=np.float32)
    mk = np.asarray(inputs["L0_attn_mix_k"], dtype=np.float32)
    mv = np.asarray(inputs["L0_attn_mix_v"], dtype=np.float32)
    mr = np.asarray(inputs["L0_attn_mix_r"], dtype=np.float32)
    tf = np.asarray(inputs["L0_attn_time_first"], dtype=np.float32)
    td = np.asarray(inputs["L0_attn_time_decay"], dtype=np.float32)
    lut = np.asarray(inputs["L0_ffn_lut"], dtype=np.float32)
    g_idx = np.asarray(inputs["L0_ffn_gate_idx"], dtype=np.int32)
    u_idx = np.asarray(inputs["L0_ffn_up_idx"], dtype=np.int32)
    d_idx = np.asarray(inputs["L0_ffn_down_idx"], dtype=np.int32)

    vecs = np.stack([scale, bias, mk, mv, mr, tf, td]).reshape(7, HI, 128)
    vecs = np.ascontiguousarray(vecs.transpose(0, 2, 1))
    grep = np.ascontiguousarray(np.tile(scale.astype(ml_dtypes.bfloat16)[None, :], (128, 1)))
    brep = np.ascontiguousarray(np.tile(bias.astype(ml_dtypes.bfloat16)[None, :], (128, 1)))
    lutb = np.ascontiguousarray(np.tile(lut[None, :], (128, 1)))
    ramp0 = np.ascontiguousarray(np.tile(np.arange(TB, dtype=np.float32)[None, :], (128, 1)))
    zrow = np.zeros((1, H), np.float32)

    in_maps = []
    for c in range(NCORE):
        b = c % 4
        sl = slice(c * 256, (c + 1) * 256)
        in_maps.append({
            "xh": np.ascontiguousarray(np.concatenate([zrow, x[b]], axis=0)),
            "wsl": np.ascontiguousarray(np.stack([Wk[sl], Wv[sl], Wr[sl], Wo[sl]])),
            "vecs": vecs,
            "grep_in": grep,
            "brep_in": brep,
            "lutb_in": lutb,
            "maskv_in": np.zeros((128, 1), np.float32),
            "ramp0_in": ramp0,
            "gidx": np.ascontiguousarray(g_idx[:, sl]),
            "uidx": np.ascontiguousarray(u_idx[:, sl]),
            "didx": np.ascontiguousarray(d_idx[sl, :]),
        })
    return in_maps


def kernel(**inputs):
    in_maps = make_in_maps(inputs)
    nc = _get_nc(dbg=False)
    res = run_bass_kernel_spmd(nc, in_maps, core_ids=list(range(NCORE)))
    yf = np.empty((4, TPC, H), np.float32)
    for c in range(4):
        yf[c] = res.results[c]["y"]
    return yf


# version 03 (final)
# speedup vs baseline: 1.2646x; 1.2646x over previous
from collections import defaultdict
from contextlib import ExitStack

import numpy as np
import ml_dtypes
import concourse.bass as bass
import concourse.mybir as mybir
from concourse.bass_utils import run_bass_kernel_spmd

F32 = mybir.dt.float32
BF16 = mybir.dt.bfloat16
I16 = mybir.dt.int16
I32 = mybir.dt.int32
AF = mybir.ActivationFunctionType
ALU = mybir.AluOpType
AXX = mybir.AxisListType.X
AP = bass.AP

H = 2048
HI = 16
NCORE = 8
TPC = 2048
NBLK = TPC // 512
TB = 512
TBP = 528
G = 2
NG = HI // G
NTT = TB // 128
FFN_O = 5504
OB = FFN_O // 128
LN_EPS = 1e-5
TILE = 128 * 128
FFN_TQ = 1024
NHALF = TPC // FFN_TQ

BLK2 = 128 * 256
A_SZ = 4 * 16 * BLK2
G_BASE = A_SZ
GU_SZ = OB * BLK2
U_BASE = G_BASE + GU_SZ
D_BASE = U_BASE + GU_SZ
D_SZ = OB * BLK2
RANK_SZ = D_BASE + D_SZ


class Prog:
    def __init__(self):
        self.events = []
        self.cnt = defaultdict(int)

    def ev(self, eng, fn, waits=(), inc=None, n=1):
        rw = [(s, (self.cnt[s] if v is None else v)) for (s, v) in waits]
        rw = [(s, v) for (s, v) in rw if v > 0]
        if inc is not None:
            self.cnt[inc] += n
        self.events.append((eng, fn, rw, inc, n))

    def barrier(self):
        bw = [(s, None) for s in list(self.cnt.keys())]
        for eng in ("sync", "gpsimd", "vector", "scalar", "tensor"):
            self.ev(eng, None, waits=bw)

    def emit(self, eng_name, eng, sems):
        last = {}
        for (e, fn, rw, inc, n) in self.events:
            if e != eng_name:
                continue
            for (s, v) in rw:
                if last.get(s, 0) < v:
                    eng.wait_ge(sems[s], v)
                    last[s] = v
            inst = fn(eng) if fn is not None else None
            if inc is not None:
                assert inst is not None
                inst.then_inc(sems[inc], n)


def build_program(dbg=False):
    nc = bass.Bass("TRN2", target_bir_lowering=False, debug=False)

    xh = nc.dram_tensor("xh", [TPC + 1, H], F32, kind="ExternalInput").ap()
    wsl = nc.dram_tensor("wsl", [4, 256, H], F32, kind="ExternalInput").ap()
    vecs = nc.dram_tensor("vecs", [7, 128, HI], F32, kind="ExternalInput").ap()
    grep_in = nc.dram_tensor("grep_in", [128, H], BF16, kind="ExternalInput").ap()
    brep_in = nc.dram_tensor("brep_in", [128, H], BF16, kind="ExternalInput").ap()
    lutb_in = nc.dram_tensor("lutb_in", [128, 16], F32, kind="ExternalInput").ap()
    maskv_in = nc.dram_tensor("maskv_in", [128, 1], F32, kind="ExternalInput").ap()
    ramp0_in = nc.dram_tensor("ramp0_in", [128, TB], F32, kind="ExternalInput").ap()
    gidx = nc.dram_tensor("gidx", [FFN_O, 256], I32, kind="ExternalInput").ap()
    uidx = nc.dram_tensor("uidx", [FFN_O, 256], I32, kind="ExternalInput").ap()
    didx = nc.dram_tensor("didx", [256, FFN_O], I32, kind="ExternalInput").ap()
    y = nc.dram_tensor("y", [TPC, H], F32, kind="ExternalOutput").ap()

    cc_w_in = nc.dram_tensor("cc_w_in", [RANK_SZ], BF16)
    cc_w_out = nc.dram_tensor("cc_w_out", [NCORE * RANK_SZ], BF16)

    def idram(name, shape, dt):
        kind = "ExternalOutput" if dbg else "Internal"
        if dbg:
            return nc.dram_tensor(name, shape, dt, kind="ExternalOutput")
        return nc.dram_tensor(name, shape, dt)

    spA = idram("spA", [NBLK, HI, 128, TB + 1], F32)
    spB = idram("spB", [NBLK, HI, 128, TB + 1], F32)
    spF = idram("spF", [NBLK, HI, 128, TB], F32)
    spFV = idram("spFV", [NBLK, HI, 128, TB], F32)
    spR = idram("spR", [NBLK, HI, 128, TB], BF16)
    spX = idram("spX", [TPC, H], F32)
    spXB = idram("spXB", [TPC, H], BF16)
    if dbg:
        spRW = idram("spRW", [NBLK, 128, HI, TB], BF16)
        spL2 = idram("spL2", [NBLK, 128, NTT, H], F32)
        spST = idram("spST", [128, 40], F32)
        spK = idram("spK", [NBLK, HI, 128, TB], F32)
        spV = idram("spV", [NBLK, HI, 128, TB], F32)
        spWKV = idram("spWKV", [NBLK, HI, 128, TB], F32)
        spWI = idram("spWI", [RANK_SZ], BF16)
        spWO = idram("spWO", [4 * RANK_SZ], BF16)
        spXN = idram("spXN", [2, 128, HI, TBP], BF16)
        spKX = idram("spKX", [128, HI, TB], BF16)

    p = Prog()
    SEM_NAMES = ["s_x", "s_tr", "s_gp", "s_mm", "s_ve", "s_ac", "s_cc",
                 "s_st", "s_sp", "s_xr", "s_w0", "s_w1", "s_w2", "s_w3"]

    with ExitStack() as ctx:
        def sb(name, shape, dt=F32):
            return ctx.enter_context(nc.sbuf_tensor(name, shape, dt))

        vec_s = sb("vec_s", [128, 7 * HI])
        mkv = vec_s[:, 2 * HI:3 * HI]; mvv = vec_s[:, 3 * HI:4 * HI]
        mrv = vec_s[:, 4 * HI:5 * HI]; uu = vec_s[:, 5 * HI:6 * HI]
        td = vec_s[:, 6 * HI:7 * HI]
        lutb = sb("lutb", [128, 16])
        maskv = sb("maskv", [128, 1])
        wdec = sb("wdec", [128, HI])
        ddec = sb("ddec", [128, HI])
        dec512 = sb("dec512", [128, HI])
        onemr = sb("onemr", [128, HI])
        pneg = sb("pneg", [128, NBLK * HI])
        ubias = sb("ubias", [128, NBLK * HI])
        aend = sb("aend", [128, HI]); bend = sb("bend", [128, HI])
        ainit = sb("ainit", [128, HI]); binit = sb("binit", [128, HI])
        ain = sb("ain", [128, HI]); bin_ = sb("bin_", [128, HI])
        tmp16 = sb("tmp16", [128, HI])
        tmp16b = sb("tmp16b", [128, HI])
        sx = sb("sx", [128, 4, 4])
        sy = sb("sy", [128, 4, 4])
        lnsx = sb("lnsx", [128, 4, 4])
        lnsy = sb("lnsy", [128, 4, 4])
        stlog = sb("stlog", [128, 40])
        epsv = sb("epsv", [128, 1])

        sems = {s: ctx.enter_context(nc.semaphore(s)) for s in SEM_NAMES}

        class Arena:
            def __init__(self, base):
                self.base = base
                self.off = base
                self.top = base

            def sbuf_tensor(self, name, shape, dt):
                per_part = int(np.prod(shape[1:])) * mybir.dt.size(dt)
                t = nc.alloc_sbuf_tensor_at(name, shape, dt, offset=self.off)
                self.off = (self.off + per_part + 31) & ~31
                self.top = max(self.top, self.off)
                return t

            def allocate(self):
                pass

            def free(self):
                pass

        ARENA_BYTES = 203 * 1024
        arena_backing = ctx.enter_context(
            nc.sbuf_tensor("arena_backing", [128, ARENA_BYTES], mybir.dt.uint8))
        ARENA_BASE = nc.lookup_mloc(arena_backing).addr
        LIMIT = ARENA_BASE + ARENA_BYTES

        arW = Arena(ARENA_BASE)
        idxs = arW.sbuf_tensor("idxs", [128, OB * 128], I16)
        idxT = arW.sbuf_tensor("idxT", [128, OB * 128], I16)
        wterm = [arW.sbuf_tensor(f"wterm{i}", [128, OB * 128], BF16) for i in range(2)]
        awt = arW.sbuf_tensor("awt", [128, H], F32)
        awb = arW.sbuf_tensor("awb", [128, H], BF16)
        assert arW.top <= LIMIT, arW.top

        ar1 = Arena(ARENA_BASE)
        xt = [ar1.sbuf_tensor(f"xt{i}", [128, H], F32) for i in range(4)]
        xnb = [ar1.sbuf_tensor(f"xnb{i}", [128, H], BF16) for i in range(4)]
        st1 = ar1.sbuf_tensor("st1", [128, H], F32)
        grep = ar1.sbuf_tensor("grep", [128, H], BF16)
        brep = ar1.sbuf_tensor("brep", [128, H], BF16)
        xncm = [ar1.sbuf_tensor(f"xncm{i}", [128, HI, TBP], BF16) for i in range(2)]
        xsc = ar1.sbuf_tensor("xsc", [128, HI, 128], BF16)
        kx = ar1.sbuf_tensor("kx", [128, HI, TB], BF16)
        vx = ar1.sbuf_tensor("vx", [128, HI, TB], BF16)
        wa = [ar1.sbuf_tensor(f"wa{i}", [128, HI * 128], BF16) for i in range(2)]
        kg = [ar1.sbuf_tensor(f"kg{i}", [128, G, TB], F32) for i in range(2)]
        vg = [ar1.sbuf_tensor(f"vg{i}", [128, G, TB], F32) for i in range(2)]
        sigr = ar1.sbuf_tensor("sigr", [128, HI, TB], BF16)
        Ee = ar1.sbuf_tensor("Ee", [128, G, TB], F32)
        Ff = ar1.sbuf_tensor("Ff", [128, G, TB], F32)
        FVv = ar1.sbuf_tensor("FVv", [128, G, TB], F32)
        scA = ar1.sbuf_tensor("scA", [128, G, TB + 1], F32)
        scB = ar1.sbuf_tensor("scB", [128, G, TB + 1], F32)
        assert ar1.top <= LIMIT, ar1.top

        ar2 = Arena(ARENA_BASE)
        Dsh = ar2.sbuf_tensor("Dsh", [128, HI, TB], F32)
        ramp0b = ar2.sbuf_tensor("ramp0b", [128, TB], F32)
        pA = ar2.sbuf_tensor("pA", [128, G, TB + 1], F32)
        pB = ar2.sbuf_tensor("pB", [128, G, TB + 1], F32)
        pF = ar2.sbuf_tensor("pF", [128, G, TB], F32)
        pFV = ar2.sbuf_tensor("pFV", [128, G, TB], F32)
        pR = ar2.sbuf_tensor("pR", [128, G, TB], BF16)
        pT1 = ar2.sbuf_tensor("pT1", [128, G, TB], F32)
        pT2 = ar2.sbuf_tensor("pT2", [128, G, TB], F32)
        rwkv = ar2.sbuf_tensor("rwkv", [128, HI, TB], BF16)
        wo = [ar2.sbuf_tensor(f"wo{i}", [128, 4 * 128], BF16) for i in range(2)]
        xr = ar2.sbuf_tensor("xr", [128, NTT, H], F32)
        l2buf = ar2.sbuf_tensor("l2buf", [128, NTT, H], F32)
        xlb2 = [ar2.sbuf_tensor(f"xlb2{i}", [128, H], BF16) for i in range(2)]
        l2o = [ar2.sbuf_tensor(f"l2o{i}", [128, H], F32) for i in range(2)]
        assert ar2.top <= LIMIT, ar2.top

        ar3 = Arena(ARENA_BASE)
        xlcm = ar3.sbuf_tensor("xlcm", [128, HI, FFN_TQ], BF16)
        hcm = ar3.sbuf_tensor("hcm", [128, OB, FFN_TQ], BF16)
        wgb = [ar3.sbuf_tensor(f"wgb{i}", [128, HI * 128], BF16) for i in range(2)]
        wub = [ar3.sbuf_tensor(f"wub{i}", [128, HI * 128], BF16) for i in range(2)]
        wdb = [ar3.sbuf_tensor(f"wdb{i}", [128, 4 * 128], BF16) for i in range(2)]
        gate_s = ar3.sbuf_tensor("gate_s", [128, TB], BF16)
        l3buf = ar3.sbuf_tensor("l3buf", [128, NTT, H], F32)
        xlT = [ar3.sbuf_tensor(f"xlT{i}", [128, TB], F32) for i in range(2)]
        xlb3 = [ar3.sbuf_tensor(f"xlb3{i}", [128, H], BF16) for i in range(2)]
        yt = [ar3.sbuf_tensor(f"yt{i}", [128, H], F32) for i in range(2)]
        sc3 = yt[0]
        assert ar3.top <= LIMIT, ar3.top

        psum = [ctx.enter_context(nc.psum_tensor(f"ps{i}", [128, 512], F32))
                for i in range(8)]
        block = ctx.enter_context(nc.Block())

        p.ev("sync", lambda e: e.dma_start(out=vec_s[:].rearrange("p (a c) -> p a c", a=7), in_=AP(vecs.tensor, 0, [[HI, 128], [128 * HI, 7], [1, HI]])), inc="s_x", n=16)
        p.ev("sync", lambda e: e.dma_start(out=lutb[:], in_=lutb_in), inc="s_x", n=16)
        p.ev("sync", lambda e: e.dma_start(out=maskv[:], in_=maskv_in), inc="s_x", n=16)
        p.ev("scalar", lambda e: e.activation(out=tmp16[:], in_=td, func=AF.Exp), waits=[("s_x", None)], inc="s_ac")
        p.ev("vector", lambda e: e.tensor_scalar_mul(out=wdec[:], in0=tmp16[:], scalar1=-1.0), waits=[("s_ac", None)], inc="s_ve")
        p.ev("scalar", lambda e: e.activation(out=ddec[:], in_=wdec[:], func=AF.Exp), waits=[("s_ve", None)], inc="s_ac")
        p.ev("scalar", lambda e: e.activation(out=dec512[:], in_=wdec[:], func=AF.Exp, scale=float(TB)), inc="s_ac")
        p.ev("vector", lambda e: e.tensor_scalar(out=onemr[:], in0=mrv, scalar1=-1.0, scalar2=1.0, op0=ALU.mult, op1=ALU.add), waits=[("s_x", None)], inc="s_ve")
        for buf in (ainit, binit, ain, bin_):
            p.ev("vector", lambda e, buf=buf: e.memset(buf[:], 0.0), inc="s_ve")
        p.ev("vector", lambda e: e.memset(epsv[:], LN_EPS), inc="s_ve")

        for m in range(4):
            for rt in range(2):
                p.ev("sync", lambda e, m=m, rt=rt: e.dma_start(out=awt[:], in_=wsl[m, rt * 128:(rt + 1) * 128, :]),
                     waits=[("s_gp", None)], inc="s_x", n=16)
                p.ev("vector", lambda e: e.tensor_copy(out=awb[:], in_=awt[:]), waits=[("s_x", None)], inc="s_ve")
                off = m * 16 * BLK2 + rt * 128
                p.ev("gpsimd", lambda e, off=off: e.dma_start(
                    out=AP(cc_w_in, off, [[256, 128], [BLK2, 16], [1, 128]]),
                    in_=awb[:].rearrange("p (t c) -> p t c", t=16)),
                    waits=[("s_ve", None)], inc="s_gp", n=16)

        def therm_stripe(cast_src_fn, acc_base, acc_ob_stride):
            p.ev("gpsimd", lambda e, f=cast_src_fn: e.dma_start(
                out=idxs[:].rearrange("p (b c) -> p b c", b=OB), in_=f()),
                waits=[("s_tr", None), ("s_gp", None)], inc="s_gp", n=16)
            for ob in range(OB):
                p.ev("sync", lambda e, ob=ob: e.dma_start(
                    out=idxT[:, ob * 128:(ob + 1) * 128],
                    in_=idxs[:, ob * 128:(ob + 1) * 128], transpose=True),
                    waits=[("s_gp", None), ("s_ve", None)], inc="s_tr", n=16)
            for j in range(16):
                wb = wterm[j % 2]
                p.ev("vector", lambda e, j=j, wb=wb: e.tensor_scalar(
                    out=wb[:], in0=idxT[:], scalar1=float(j), scalar2=lutb[:, j:j + 1],
                    op0=ALU.is_equal, op1=ALU.mult),
                    waits=[("s_tr", None), ("s_gp", None)], inc="s_ve")
                kw = {} if j == 0 else {"accum_op": ALU.add}
                p.ev("gpsimd", lambda e, wb=wb, kw=kw, base=acc_base, stp=acc_ob_stride: e.dma_start(
                    out=AP(cc_w_in, base, [[256, 128], [stp, OB], [1, 128]]),
                    in_=wb[:].rearrange("p (b c) -> p b c", b=OB), **kw),
                    waits=[("s_ve", None)], inc="s_gp", n=16)

        for (mat_t, mbase) in ((gidx.tensor, G_BASE), (uidx.tensor, U_BASE)):
            for ht in range(2):
                therm_stripe(
                    lambda mat_t=mat_t, ht=ht: AP(mat_t, ht * 128,
                                                  [[256, 128], [128 * 256, OB], [1, 128]]),
                    mbase + ht * 128, BLK2)
        for dt2 in range(2):
            therm_stripe(
                lambda dt2=dt2: AP(didx.tensor, dt2 * 128 * FFN_O,
                                   [[FFN_O, 128], [128, OB], [1, 128]]),
                D_BASE + dt2 * 128, BLK2)

        if dbg:
            p.ev("gpsimd", lambda e: e.dma_start(out=spWI.ap(), in_=cc_w_in.ap()),
                 waits=[("s_gp", None)], inc="s_gp", n=16)
        p.ev("gpsimd", lambda e: e.collective_compute(
            "AllGather", ALU.bypass, replica_groups=[list(range(NCORE))],
            ins=[cc_w_in.ap().opt()], outs=[cc_w_out.ap().opt()]),
            waits=[("s_gp", None)], inc="s_cc")
        if dbg:
            p.ev("gpsimd", lambda e: e.dma_start(
                out=spWO.ap(), in_=AP(cc_w_out, 0, [[1, 4 * RANK_SZ]])),
                waits=[("s_cc", None)], inc="s_gp", n=16)

        p.barrier()

        p.ev("sync", lambda e: e.dma_start(out=grep[:], in_=grep_in), inc="s_x", n=16)
        p.ev("sync", lambda e: e.dma_start(out=brep[:], in_=brep_in), inc="s_x", n=16)

        ln1_ctr = [0]
        pend = []

        def ln1_stats(rows0, bp, par, nr=128):
            p.ev("sync", lambda e, rows0=rows0, bp=bp, nr=nr: e.dma_start(
                out=xt[bp][:nr, :], in_=xh[rows0:rows0 + nr, :]),
                waits=[("s_ve", None)], inc="s_xr", n=16)
            p.ev("vector", lambda e, bp=bp, par=par, nr=nr: e.tensor_scalar(
                out=st1[:nr, :], in0=xt[bp][:nr, :], scalar1=1.0, scalar2=0.0,
                op0=ALU.mult, op1=ALU.add, accum_out=sx[:nr, par, 0:1]),
                waits=[("s_xr", None)], inc="s_ve")
            p.ev("vector", lambda e, bp=bp, par=par, nr=nr: e.scalar_tensor_tensor(
                out=st1[:nr, :], in0=xt[bp][:nr, :], scalar=1.0, in1=xt[bp][:nr, :],
                op0=ALU.mult, op1=ALU.mult, accum_out=sx[:nr, par, 1:2]), inc="s_ve")

        def ln1_apply(bp, par, tr_fn, nr=128, ti=None):
            p.ev("scalar", lambda e, par=par, nr=nr: e.activation(
                out=sy[:nr, par, 0:1], in_=sx[:nr, par, 0:1], func=AF.Identity,
                scale=1.0 / H), waits=[("s_ve", None)], inc="s_ac")
            p.ev("scalar", lambda e, par=par, nr=nr: e.activation(
                out=sx[:nr, par, 2:3], in_=sx[:nr, par, 0:1], func=AF.Square,
                scale=1.0 / H), inc="s_ac")
            p.ev("gpsimd", lambda e, par=par, nr=nr: e.tensor_scalar(
                out=sy[:nr, par, 2:3], in0=sx[:nr, par, 1:2], scalar1=1.0 / H,
                scalar2=sx[:nr, par, 2:3], op0=ALU.mult, op1=ALU.subtract),
                waits=[("s_ac", None), ("s_ve", None)], inc="s_gp")
            p.ev("scalar", lambda e, par=par, nr=nr: e.activation(
                out=sy[:nr, par, 2:3], in_=sy[:nr, par, 2:3], func=AF.Ln, bias=epsv[:nr, :]),
                waits=[("s_gp", None)], inc="s_ac")
            p.ev("scalar", lambda e, par=par, nr=nr: e.activation(
                out=sy[:nr, par, 2:3], in_=sy[:nr, par, 2:3], func=AF.Exp, scale=-0.5), inc="s_ac")
            if dbg and ti is not None and ti < 5:
                p.ev("vector", lambda e, ti=ti, par=par, nr=nr: e.tensor_copy(
                    out=stlog[:nr, 4 * ti + 3:4 * ti + 4], in_=sy[:nr, par, 2:3]),
                    waits=[("s_ac", None)], inc="s_ve")
            p.ev("vector", lambda e, bp=bp, par=par, nr=nr: e.tensor_scalar(
                out=st1[:nr, :], in0=xt[bp][:nr, :], scalar1=sy[:nr, par, 0:1], scalar2=sy[:nr, par, 2:3],
                op0=ALU.subtract, op1=ALU.mult), waits=[("s_ac", None)], inc="s_ve")
            p.ev("vector", lambda e, bp=bp, nr=nr: e.tensor_tensor(
                out=xnb[bp][:nr, :], in0=st1[:nr, :], in1=grep[:nr, :], op=ALU.mult),
                waits=[("s_tr", None)], inc="s_ve")
            p.ev("vector", lambda e, bp=bp, nr=nr: e.tensor_tensor(
                out=xnb[bp][:nr, :], in0=xnb[bp][:nr, :], in1=brep[:nr, :], op=ALU.add), inc="s_ve")
            tr_fn(bp)

        def ln1_tile(rows0, bp, tr_fn, nr=128):
            ti = ln1_ctr[0]; ln1_ctr[0] += 1
            par = ti % 4
            ln1_stats(rows0, par, par, nr)
            if len(pend) >= 2:
                ln1_apply(*pend.pop(0))
            pend.append((par, par, tr_fn, nr, ti))

        def ln1_flush():
            p.ev("vector", lambda e: e.drain(), inc="s_ve")
            p.ev("vector", lambda e: e.nop(cycle_cnt=2000), inc=None) if False else None
            p.ev("vector", lambda e: e.drain(), inc="s_ve")
            while pend:
                ln1_apply(*pend.pop(0))

        def tr_to(buf, c0):
            def f(bp):
                for hi in range(HI):
                    p.ev("sync", lambda e, bp=bp, hi=hi, buf=buf, c0=c0: e.dma_start(
                        out=buf[:, hi, c0:c0 + 128],
                        in_=xnb[bp][:, hi * 128:(hi + 1) * 128], transpose=True),
                        waits=[("s_ve", None)], inc="s_tr", n=16)
            return f

        def tr_to_scratch(bp):
            for hi in range(HI):
                p.ev("sync", lambda e, bp=bp, hi=hi: e.dma_start(
                    out=xsc[:, hi, :],
                    in_=xnb[bp][:, hi * 128:(hi + 1) * 128], transpose=True),
                    waits=[("s_ve", None)], inc="s_tr", n=16)

        bank_free = [None] * 8

        def mm_group(bank, pairs, extra_waits=()):
            w = list(extra_waits)
            if bank_free[bank] is not None:
                w.append(bank_free[bank])
            n = len(pairs)
            for i, (lf, rf) in enumerate(pairs):
                p.ev("tensor", lambda e, lf=lf, rf=rf, i=i, n=n, bank=bank: e.matmul(
                    psum[bank][:], lf(), rf(), start=(i == 0), stop=(i == n - 1)),
                    waits=(w if i == 0 else ()), inc=("s_mm" if i == n - 1 else None))
            return ("s_mm", p.cnt["s_mm"])

        def wa_load(mat, dt, par):
            off = mat * 16 * BLK2 + dt * BLK2
            p.ev("sync", lambda e, off=off, par=par: e.dma_start(
                out=wa[par][:],
                in_=AP(cc_w_out, off, [[256, 128], [RANK_SZ, 8], [1, 256]])),
                waits=[("s_cc", None), ("s_mm", None)], inc="s_x", n=16)

        def ln1_block(b):
            buf = xncm[b % 2]
            for t in range(NTT):
                ln1_tile(b * TB + t * 128, t % 2, tr_to(buf, t * 128))
            if b == NBLK - 1:
                ln1_tile(TPC - 127, 0, tr_to_scratch)
                ln1_flush()

        def compute_block(b):
            buf = xncm[b % 2]
            if b < NBLK - 1:
                nbuf = xncm[(b + 1) % 2]
                p.ev("vector", lambda e, buf=buf, nbuf=nbuf: e.tensor_copy(
                    out=buf[:, :, TB], in_=nbuf[:, :, 0]),
                    waits=[("s_tr", None)], inc="s_ve")
            else:
                p.ev("vector", lambda e, buf=buf: e.tensor_copy(
                    out=buf[:, :, TB], in_=xsc[:, :, 127]),
                    waits=[("s_tr", None)], inc="s_ve")
            if b == 0:
                p.ev("gpsimd", lambda e, buf=buf: e.tensor_copy(
                    out=tmp16[:], in_=buf[:, :, 0]),
                    waits=[("s_tr", None)], inc="s_gp")
                p.ev("gpsimd", lambda e, buf=buf: e.tensor_scalar_mul(
                    out=buf[:, :, 0], in0=tmp16[:], scalar1=maskv[:]), inc="s_gp")
            x1 = buf[:, :, 1:TB + 1]
            x0 = buf[:, :, 0:TB]
            p.ev("vector", lambda e, x1=x1, x0=x0: e.tensor_tensor(
                out=vx[:], in0=x1, in1=x0, op=ALU.subtract),
                waits=[("s_tr", None), ("s_mm", None), ("s_gp", None)], inc="s_ve")
            for hi in range(HI):
                p.ev("vector", lambda e, hi=hi, x0=x0: e.scalar_tensor_tensor(
                    out=kx[:, hi, :], in0=vx[:, hi, :], scalar=mkv[:, hi:hi + 1],
                    in1=x0[:, hi, :], op0=ALU.mult, op1=ALU.add), inc="s_ve")
            for hi in range(HI):
                p.ev("vector", lambda e, hi=hi, x0=x0: e.scalar_tensor_tensor(
                    out=vx[:, hi, :], in0=vx[:, hi, :], scalar=mvv[:, hi:hi + 1],
                    in1=x0[:, hi, :], op0=ALU.mult, op1=ALU.add), inc="s_ve")
            mix_done = ("s_ve", p.cnt["s_ve"])
            if dbg and b < 2:
                p.ev("sync", lambda e, b=b, buf=buf: e.dma_start(
                    out=AP(spXN, b * 128 * HI * TBP,
                           [[HI * TBP, 128], [TBP, HI], [1, TBP]]),
                    in_=buf[:]), waits=[("s_ve", None)], inc="s_sp", n=16)
            if dbg and b == 0:
                p.ev("sync", lambda e: e.dma_start(
                    out=AP(spKX, 0, [[HI * TB, 128], [TB, HI], [1, TB]]),
                    in_=kx[:]), waits=[("s_ve", None)], inc="s_sp", n=16)

            for dt in range(HI):
                par = dt % 2
                gp = (dt // G) % 2
                j = dt % G
                wa_load(0, dt, par)
                fb = mm_group(dt % 2, [
                    (lambda kt=kt, par=par: wa[par][:, kt * 128:(kt + 1) * 128],
                     lambda kt=kt: kx[:, kt, :]) for kt in range(HI)],
                    extra_waits=[("s_x", None), mix_done])
                p.ev("scalar", lambda e, gp=gp, j=j, bk=dt % 2: e.copy(
                    out=kg[gp][:, j, :], in_=psum[bk][:]),
                    waits=[fb, ("s_ve", None), ("s_sp", None)], inc="s_ac")
                bank_free[dt % 2] = ("s_ac", p.cnt["s_ac"])
                if dbg:
                    p.ev("sync", lambda e, gp=gp, j=j, b=b, dt=dt: e.dma_start(
                        out=AP(spK, ((b * HI + dt) * 128) * TB, [[TB, 128], [1, TB]]),
                        in_=kg[gp][:, j, :]), waits=[("s_ac", None)], inc="s_sp", n=16)
                wa_load(1, dt, par)
                fb = mm_group(2 + dt % 2, [
                    (lambda kt=kt, par=par: wa[par][:, kt * 128:(kt + 1) * 128],
                     lambda kt=kt: vx[:, kt, :]) for kt in range(HI)],
                    extra_waits=[("s_x", None)])
                p.ev("scalar", lambda e, gp=gp, j=j, bk=2 + dt % 2: e.copy(
                    out=vg[gp][:, j, :], in_=psum[bk][:]),
                    waits=[fb, ("s_ve", None), ("s_sp", None)], inc="s_ac")
                bank_free[2 + dt % 2] = ("s_ac", p.cnt["s_ac"])
                if dbg:
                    p.ev("sync", lambda e, gp=gp, j=j, b=b, dt=dt: e.dma_start(
                        out=AP(spV, ((b * HI + dt) * 128) * TB, [[TB, 128], [1, TB]]),
                        in_=vg[gp][:, j, :]), waits=[("s_ac", None)], inc="s_sp", n=16)

                if j == G - 1:
                    g0 = dt - G + 1
                    bhi0 = b * HI + g0
                    for j2 in range(G):
                        p.ev("vector", lambda e, gp=gp, j2=j2, bhi0=bhi0: e.reduce_max(
                            out=pneg[:, bhi0 + j2:bhi0 + j2 + 1], in_=kg[gp][:, j2, :],
                            axis=AXX, negate=True), waits=[("s_ac", None)], inc="s_ve")
                    p.ev("gpsimd", lambda e, g0=g0, bhi0=bhi0: e.tensor_tensor(
                        out=ubias[:, bhi0:bhi0 + G], in0=uu[:, g0:g0 + G],
                        in1=pneg[:, bhi0:bhi0 + G], op=ALU.add),
                        waits=[("s_ve", None)], inc="s_gp")
                    if b > 0:
                        pbhi0 = (b - 1) * HI + g0
                        p.ev("gpsimd", lambda e, bhi0=bhi0, pbhi0=pbhi0, g0=g0: e.tensor_tensor(
                            out=tmp16[:, g0:g0 + G], in0=pneg[:, bhi0:bhi0 + G],
                            in1=pneg[:, pbhi0:pbhi0 + G], op=ALU.subtract), inc="s_gp")
                        p.ev("scalar", lambda e, g0=g0: e.activation(
                            out=tmp16[:, g0:g0 + G], in_=tmp16[:, g0:g0 + G], func=AF.Exp),
                            waits=[("s_gp", None)], inc="s_ac")
                        p.ev("gpsimd", lambda e, g0=g0: e.tensor_tensor(
                            out=ainit[:, g0:g0 + G], in0=aend[:, g0:g0 + G],
                            in1=tmp16[:, g0:g0 + G], op=ALU.mult),
                            waits=[("s_ac", None)], inc="s_gp")
                        p.ev("gpsimd", lambda e, g0=g0: e.tensor_tensor(
                            out=binit[:, g0:g0 + G], in0=bend[:, g0:g0 + G],
                            in1=tmp16[:, g0:g0 + G], op=ALU.mult), inc="s_gp")
                    ve_stats = p.cnt["s_ve"]
                    for j2 in range(G):
                        p.ev("scalar", lambda e, gp=gp, j2=j2, bhi0=bhi0: e.activation(
                            out=Ee[:, j2, :], in_=kg[gp][:, j2, :], func=AF.Exp,
                            bias=pneg[:, bhi0 + j2:bhi0 + j2 + 1]),
                            waits=[("s_ve", ve_stats), ("s_sp", None)], inc="s_ac")
                        p.ev("scalar", lambda e, gp=gp, j2=j2, bhi0=bhi0: e.activation(
                            out=Ff[:, j2, :], in_=kg[gp][:, j2, :], func=AF.Exp,
                            bias=ubias[:, bhi0 + j2:bhi0 + j2 + 1]), inc="s_ac")
                    for j2 in range(G):
                        hi = g0 + j2
                        p.ev("vector", lambda e, j2=j2, hi=hi: e.tensor_copy(
                            out=scB[:, j2, 0:1], in_=binit[:, hi:hi + 1]),
                            waits=[("s_ac", None), ("s_sp", None), ("s_gp", None)], inc="s_ve")
                        p.ev("vector", lambda e, j2=j2, hi=hi: e.tensor_tensor_scan(
                            out=scB[:, j2, 1:TB + 1],
                            data0=ddec[:, hi:hi + 1].broadcast_to([128, TB]),
                            data1=Ee[:, j2, :], initial=binit[:, hi:hi + 1],
                            op0=ALU.mult, op1=ALU.add), inc="s_ve")
                    p.ev("vector", lambda e, gp=gp: e.tensor_tensor(
                        out=Ee[:], in0=Ee[:], in1=vg[gp][:], op=ALU.mult), inc="s_ve")
                    p.ev("vector", lambda e, gp=gp: e.tensor_tensor(
                        out=FVv[:], in0=Ff[:], in1=vg[gp][:], op=ALU.mult), inc="s_ve")
                    for j2 in range(G):
                        hi = g0 + j2
                        p.ev("vector", lambda e, j2=j2, hi=hi: e.tensor_copy(
                            out=scA[:, j2, 0:1], in_=ainit[:, hi:hi + 1]), inc="s_ve")
                        p.ev("vector", lambda e, j2=j2, hi=hi: e.tensor_tensor_scan(
                            out=scA[:, j2, 1:TB + 1],
                            data0=ddec[:, hi:hi + 1].broadcast_to([128, TB]),
                            data1=Ee[:, j2, :], initial=ainit[:, hi:hi + 1],
                            op0=ALU.mult, op1=ALU.add), inc="s_ve")
                    p.ev("gpsimd", lambda e, g0=g0: e.tensor_copy(
                        out=aend[:, g0:g0 + G], in_=scA[:, :, TB]),
                        waits=[("s_ve", None)], inc="s_gp")
                    p.ev("gpsimd", lambda e, g0=g0: e.tensor_copy(
                        out=bend[:, g0:g0 + G], in_=scB[:, :, TB]), inc="s_gp")
                    ve_sc = p.cnt["s_ve"]
                    for (t_, sp, C) in ((scA, spA, TB + 1), (scB, spB, TB + 1),
                                        (Ff, spF, TB), (FVv, spFV, TB)):
                        p.ev("sync", lambda e, t_=t_, sp=sp, C=C, b=b, g0=g0: e.dma_start(
                            out=AP(sp, (b * HI + g0) * 128 * C, [[C, 128], [128 * C, G], [1, C]]),
                            in_=t_[:, :, 0:C] if C == TB else t_[:, :, :]),
                            waits=[("s_ve", ve_sc), ("s_ac", None)], inc="s_sp", n=16)

            kvr_done = ("s_mm", p.cnt["s_mm"])
            for hi in range(HI):
                p.ev("vector", lambda e, hi=hi, x1=x1: e.tensor_scalar_mul(
                    out=kx[:, hi, :], in0=x1[:, hi, :], scalar1=mrv[:, hi:hi + 1]),
                    waits=([kvr_done] if hi == 0 else ()), inc="s_ve")
            for hi in range(HI):
                p.ev("vector", lambda e, hi=hi, x0=x0: e.scalar_tensor_tensor(
                    out=kx[:, hi, :], in0=x0[:, hi, :], scalar=onemr[:, hi:hi + 1],
                    in1=kx[:, hi, :], op0=ALU.mult, op1=ALU.add), inc="s_ve")
            rx_done = ("s_ve", p.cnt["s_ve"])
            for dt in range(HI):
                par = dt % 2
                wa_load(2, dt, par)
                rbank = 4 + dt % 2
                fb = mm_group(rbank, [
                    (lambda kt=kt, par=par: wa[par][:, kt * 128:(kt + 1) * 128],
                     lambda kt=kt: kx[:, kt, :]) for kt in range(HI)],
                    extra_waits=[("s_x", None), rx_done])
                p.ev("scalar", lambda e, dt=dt, rbank=rbank: e.activation(
                    out=sigr[:, dt, :], in_=psum[rbank][:], func=AF.Sigmoid),
                    waits=[fb, ("s_sp", None)], inc="s_ac")
                bank_free[rbank] = ("s_ac", p.cnt["s_ac"])
            p.ev("sync", lambda e, b=b: e.dma_start(
                out=AP(spR, b * HI * 128 * TB, [[TB, 128], [128 * TB, HI], [1, TB]]),
                in_=sigr[:]), waits=[("s_ac", None)], inc="s_sp", n=16)

        for b in range(NBLK + 1):
            if b < NBLK:
                ln1_block(b)
            if b >= 1:
                compute_block(b - 1)

        p.barrier()

        p.ev("sync", lambda e: e.dma_start(out=ramp0b[:], in_=ramp0_in), inc="s_x", n=16)
        for hi in range(HI):
            p.ev("vector", lambda e, hi=hi: e.tensor_scalar_mul(
                out=Dsh[:, hi, :], in0=ramp0b[:], scalar1=wdec[:, hi:hi + 1]),
                waits=[("s_x", None)], inc="s_ve")
        p.ev("scalar", lambda e: e.activation(out=Dsh[:], in_=Dsh[:], func=AF.Exp),
             waits=[("s_ve", None)], inc="s_ac")

        for b in range(NBLK):
            if b > 0:
                p.ev("gpsimd", lambda e, b=b: e.tensor_tensor(
                    out=tmp16[:], in0=pneg[:, b * HI:(b + 1) * HI],
                    in1=pneg[:, (b - 1) * HI:b * HI], op=ALU.subtract),
                    waits=[("s_ac", None), ("s_ve", None)], inc="s_gp")
                p.ev("scalar", lambda e: e.activation(out=tmp16[:], in_=tmp16[:], func=AF.Exp),
                     waits=[("s_gp", None)], inc="s_ac")
                p.ev("gpsimd", lambda e: e.tensor_tensor(out=tmp16b[:], in0=ain[:], in1=dec512[:], op=ALU.mult),
                     waits=[("s_ac", None)], inc="s_gp")
                p.ev("gpsimd", lambda e: e.tensor_tensor(out=ain[:], in0=tmp16b[:], in1=tmp16[:], op=ALU.mult), inc="s_gp")
                p.ev("gpsimd", lambda e: e.tensor_tensor(out=tmp16b[:], in0=bin_[:], in1=dec512[:], op=ALU.mult), inc="s_gp")
                p.ev("gpsimd", lambda e: e.tensor_tensor(out=bin_[:], in0=tmp16b[:], in1=tmp16[:], op=ALU.mult), inc="s_gp")
            for g in range(NG):
                g0 = g * G
                ve_used = p.cnt["s_ve"]
                for (t_, sp, C) in ((pA, spA, TB + 1), (pB, spB, TB + 1),
                                    (pF, spF, TB), (pFV, spFV, TB)):
                    p.ev("sync", lambda e, t_=t_, sp=sp, C=C, b=b, g0=g0: e.dma_start(
                        out=t_[:, :, 0:C] if C == TB else t_[:, :, :],
                        in_=AP(sp, (b * HI + g0) * 128 * C, [[C, 128], [128 * C, G], [1, C]])),
                        waits=[("s_ve", ve_used)], inc="s_x", n=16)
                p.ev("sync", lambda e, b=b, g0=g0: e.dma_start(
                    out=pR[:],
                    in_=AP(spR, (b * HI + g0) * 128 * TB, [[TB, 128], [128 * TB, G], [1, TB]])),
                    waits=[("s_ve", ve_used)], inc="s_x", n=16)
                xw = [("s_x", None), ("s_ac", None), ("s_gp", None)]
                for j2 in range(G):
                    hi = g0 + j2
                    p.ev("vector", lambda e, j2=j2, hi=hi: e.scalar_tensor_tensor(
                        out=pT1[:, j2, :], in0=Dsh[:, hi, :], scalar=ain[:, hi:hi + 1],
                        in1=pA[:, j2, 0:TB], op0=ALU.mult, op1=ALU.add),
                        waits=(xw if j2 == 0 else ()), inc="s_ve")
                    p.ev("vector", lambda e, j2=j2, hi=hi: e.scalar_tensor_tensor(
                        out=pT2[:, j2, :], in0=Dsh[:, hi, :], scalar=bin_[:, hi:hi + 1],
                        in1=pB[:, j2, 0:TB], op0=ALU.mult, op1=ALU.add), inc="s_ve")
                p.ev("vector", lambda e: e.tensor_tensor(out=pT1[:], in0=pT1[:], in1=pFV[:], op=ALU.add), inc="s_ve")
                p.ev("vector", lambda e: e.tensor_tensor(out=pT2[:], in0=pT2[:], in1=pF[:], op=ALU.add), inc="s_ve")
                p.ev("scalar", lambda e: e.activation(out=pT2[:], in_=pT2[:], func=AF.Ln),
                     waits=[("s_ve", None)], inc="s_ac")
                p.ev("scalar", lambda e: e.activation(out=pT2[:], in_=pT2[:], func=AF.Exp, scale=-1.0), inc="s_ac")
                p.ev("vector", lambda e: e.tensor_tensor(out=pT1[:], in0=pT1[:], in1=pT2[:], op=ALU.mult),
                     waits=[("s_ac", None)], inc="s_ve")
                if dbg:
                    p.ev("sync", lambda e, b=b, g0=g0: e.dma_start(
                        out=AP(spWKV, (b * HI + g0) * 128 * TB, [[TB, 128], [128 * TB, G], [1, TB]]),
                        in_=pT1[:]), waits=[("s_ve", None)], inc="s_sp", n=16)
                p.ev("vector", lambda e, g0=g0: e.tensor_tensor(
                    out=rwkv[:, g0:g0 + G, :], in0=pT1[:], in1=pR[:], op=ALU.mult), inc="s_ve")
            rw_done = ("s_ve", p.cnt["s_ve"])
            if dbg:
                p.ev("sync", lambda e, b=b: e.dma_start(
                    out=AP(spRW, b * 128 * HI * TB, [[HI * TB, 128], [TB, HI], [1, TB]]),
                    in_=rwkv[:]), waits=[rw_done], inc="s_sp", n=16)

            for tt in range(NTT):
                p.ev("sync", lambda e, b=b, tt=tt: e.dma_start(
                    out=xr[:, tt, :], in_=xh[1 + b * TB + tt * 128: 1 + b * TB + (tt + 1) * 128, :]),
                    waits=[("s_ve", None)], inc="s_xr", n=16)

            for q in range(4):
                m0 = p.cnt["s_mm"]
                for kt in range(HI):
                    parw = kt % 2
                    off = (kt // 2) * RANK_SZ + 3 * 16 * BLK2 + q * 4 * BLK2 + (kt % 2) * 128
                    lw = m0 if kt < 2 else m0 + kt - 1
                    p.ev("sync", lambda e, off=off, parw=parw: e.dma_start(
                        out=wo[parw][:].rearrange("p (t c) -> p t c", t=4),
                        in_=AP(cc_w_out, off, [[256, 128], [BLK2, 4], [1, 128]])),
                        waits=[("s_mm", lw)], inc=f"s_w{parw}", n=16)
                    wv = ("s_w%d" % parw, p.cnt["s_w%d" % parw])
                    for tt in range(4):
                        w = []
                        if tt == 0:
                            w = [wv]
                        if kt == 0:
                            w = [wv, rw_done]
                            if bank_free[4 + tt] is not None:
                                w.append(bank_free[4 + tt])
                        p.ev("tensor", lambda e, kt=kt, tt=tt, parw=parw: e.matmul(
                            psum[4 + tt][:], rwkv[:, kt, tt * 128:(tt + 1) * 128],
                            wo[parw][:], start=(kt == 0), stop=(kt == HI - 1)),
                            waits=w, inc=("s_mm" if tt == 3 else None))
                mmv = p.cnt["s_mm"]
                for tt in range(4):
                    p.ev("vector", lambda e, tt=tt, q=q: e.scalar_tensor_tensor(
                        out=l2buf[:, tt, q * 512:(q + 1) * 512], in0=xr[:, tt, q * 512:(q + 1) * 512],
                        scalar=2.0, in1=psum[4 + tt][:], op0=ALU.mult, op1=ALU.add),
                        waits=[("s_mm", mmv), ("s_xr", None), ("s_sp", None)], inc="s_ve")
                p.ev("vector", lambda e: e.nop(), inc="s_ve")
                p.ev("vector", lambda e: e.nop(), inc="s_ve")
                for tt in range(4):
                    bank_free[4 + tt] = ("s_ve", p.cnt["s_ve"])

            if dbg:
                p.ev("sync", lambda e, b=b: e.dma_start(
                    out=AP(spL2, b * 128 * NTT * H, [[NTT * H, 128], [H, NTT], [1, H]]),
                    in_=l2buf[:]), waits=[("s_ve", None)], inc="s_sp", n=16)
            for tt in range(NTT):
                p.ev("vector", lambda e, tt=tt: e.tensor_scalar(
                    out=xr[:, 0, :], in0=l2buf[:, tt, :], scalar1=1.0,
                    scalar2=0.0, op0=ALU.mult, op1=ALU.add, accum_out=lnsx[:, tt, 0:1]), inc="s_ve")
                p.ev("vector", lambda e, tt=tt: e.scalar_tensor_tensor(
                    out=xr[:, 0, :],
                    in0=l2buf[:, tt, :], scalar=1.0, in1=l2buf[:, tt, :],
                    op0=ALU.mult, op1=ALU.mult, accum_out=lnsx[:, tt, 1:2]), inc="s_ve")
            for tt in range(NTT):
                p.ev("scalar", lambda e, tt=tt: e.activation(
                    out=lnsy[:, tt, 0:1], in_=lnsx[:, tt, 0:1], func=AF.Identity,
                    scale=1.0 / H), waits=[("s_ve", None)], inc="s_ac")
                p.ev("scalar", lambda e, tt=tt: e.activation(
                    out=lnsx[:, tt, 2:3], in_=lnsx[:, tt, 0:1], func=AF.Square,
                    scale=1.0 / H), inc="s_ac")
                p.ev("gpsimd", lambda e, tt=tt: e.tensor_scalar(
                    out=lnsy[:, tt, 2:3], in0=lnsx[:, tt, 1:2], scalar1=1.0 / H,
                    scalar2=lnsx[:, tt, 2:3], op0=ALU.mult, op1=ALU.subtract),
                    waits=[("s_ac", None), ("s_ve", None)], inc="s_gp")
                p.ev("scalar", lambda e, tt=tt: e.activation(out=lnsy[:, tt, 2:3], in_=lnsy[:, tt, 2:3], func=AF.Ln, bias=epsv[:]), waits=[("s_gp", None)], inc="s_ac")
                p.ev("scalar", lambda e, tt=tt: e.activation(out=lnsy[:, tt, 2:3], in_=lnsy[:, tt, 2:3], func=AF.Exp, scale=-0.5), inc="s_ac")
                p.ev("vector", lambda e, tt=tt: e.tensor_scalar(
                    out=l2o[tt % 2][:], in0=l2buf[:, tt, :], scalar1=lnsy[:, tt, 0:1], scalar2=lnsy[:, tt, 2:3],
                    op0=ALU.subtract, op1=ALU.mult),
                    waits=[("s_ac", None), ("s_sp", None)], inc="s_ve")
                p.ev("vector", lambda e, tt=tt: e.tensor_copy(
                    out=xlb2[tt % 2][:], in_=l2o[tt % 2][:]),
                    waits=[("s_sp", None)], inc="s_ve")
                rows0 = b * TB + tt * 128
                p.ev("sync", lambda e, tt=tt, rows0=rows0: e.dma_start(
                    out=AP(spX, rows0 * H, [[H, 128], [1, H]]), in_=l2o[tt % 2][:]),
                    waits=[("s_ve", None)], inc="s_sp", n=16)
                p.ev("sync", lambda e, tt=tt, rows0=rows0: e.dma_start(
                    out=AP(spXB, rows0 * H, [[H, 128], [1, H]]), in_=xlb2[tt % 2][:]),
                    waits=[("s_ve", None)], inc="s_sp", n=16)

        p.barrier()

        for hh in range(NHALF):
            for i in range(FFN_TQ // 128):
                rows0 = hh * FFN_TQ + i * 128
                bp = i % 2
                p.ev("sync", lambda e, rows0=rows0, bp=bp: e.dma_start(
                    out=xlb3[bp][:], in_=AP(spXB, rows0 * H, [[H, 128], [1, H]])),
                    waits=[("s_tr", None), ("s_sp", None), ("s_mm", None)], inc="s_x", n=16)
                for hi in range(HI):
                    p.ev("sync", lambda e, bp=bp, hi=hi, i=i: e.dma_start(
                        out=xlcm[:, hi, i * 128:(i + 1) * 128],
                        in_=xlb3[bp][:, hi * 128:(hi + 1) * 128], transpose=True),
                        waits=[("s_x", None)], inc="s_tr", n=16)
            tr_fill = ("s_tr", p.cnt["s_tr"])

            m0g = p.cnt["s_mm"]
            for ot in range(OB):
                par = ot % 2
                lw = m0g if ot < 2 else m0g + 4 * (ot - 1)
                p.ev("sync", lambda e, ot=ot, par=par: e.dma_start(
                    out=wgb[par][:],
                    in_=AP(cc_w_out, G_BASE + ot * BLK2,
                           [[256, 128], [RANK_SZ, 8], [1, 256]])),
                    waits=[("s_mm", lw)], inc=f"s_w{par}", n=16)
                wvg = ("s_w%d" % par, p.cnt["s_w%d" % par])
                p.ev("sync", lambda e, ot=ot, par=par: e.dma_start(
                    out=wub[par][:],
                    in_=AP(cc_w_out, U_BASE + ot * BLK2,
                           [[256, 128], [RANK_SZ, 8], [1, 256]])),
                    inc=f"s_w{2 + par}", n=16)
                wvu = ("s_w%d" % (2 + par), p.cnt["s_w%d" % (2 + par)])
                for tq in range(FFN_TQ // TB):
                    gb = tq
                    ub = 2 + tq
                    fbg = mm_group(gb, [
                        (lambda kt=kt, par=par: wgb[par][:, kt * 128:(kt + 1) * 128],
                         lambda kt=kt, tq=tq: xlcm[:, kt, tq * TB:(tq + 1) * TB]) for kt in range(HI)],
                        extra_waits=[wvg, ("s_cc", None), tr_fill])
                    p.ev("scalar", lambda e, gb=gb: e.activation(
                        out=gate_s[:], in_=psum[gb][:], func=AF.Silu),
                        waits=[fbg, ("s_ve", None)], inc="s_ac")
                    bank_free[gb] = ("s_ac", p.cnt["s_ac"])
                    fbu = mm_group(ub, [
                        (lambda kt=kt, par=par: wub[par][:, kt * 128:(kt + 1) * 128],
                         lambda kt=kt, tq=tq: xlcm[:, kt, tq * TB:(tq + 1) * TB]) for kt in range(HI)],
                        extra_waits=[wvu])
                    p.ev("vector", lambda e, ot=ot, tq=tq, ub=ub: e.tensor_tensor(
                        out=hcm[:, ot, tq * TB:(tq + 1) * TB], in0=psum[ub][:],
                        in1=gate_s[:], op=ALU.mult),
                        waits=[fbu, ("s_ac", None)], inc="s_ve")
                    p.ev("vector", lambda e: e.nop(), inc="s_ve")
                    bank_free[ub] = ("s_ve", p.cnt["s_ve"])
            h_done = ("s_ve", p.cnt["s_ve"])

            for ttg in range(2):
                for q in range(4):
                    m0 = p.cnt["s_mm"]
                    for ob in range(OB):
                        parw = ob % 2
                        off = (2 * q) * RANK_SZ + D_BASE + ob * BLK2
                        lw = m0 if ob < 2 else m0 + ob - 1
                        p.ev("sync", lambda e, off=off, parw=parw: e.dma_start(
                            out=wdb[parw][:],
                            in_=AP(cc_w_out, off, [[256, 128], [RANK_SZ, 2], [1, 256]])),
                            waits=[("s_mm", lw)], inc=f"s_w{parw}", n=16)
                        wv = ("s_w%d" % parw, p.cnt["s_w%d" % parw])
                        for tl in range(4):
                            tt = ttg * 4 + tl
                            w = []
                            if tl == 0:
                                w = [wv]
                            if ob == 0:
                                w = [wv, h_done]
                                if bank_free[tl] is not None:
                                    w.append(bank_free[tl])
                            p.ev("tensor", lambda e, ob=ob, tl=tl, tt=tt, parw=parw: e.matmul(
                                psum[tl][:], hcm[:, ob, tt * 128:(tt + 1) * 128],
                                wdb[parw][:], start=(ob == 0), stop=(ob == OB - 1)),
                                waits=w, inc=("s_mm" if tl == 3 else None))
                    mmv = p.cnt["s_mm"]
                    for tl in range(4):
                        tt = ttg * 4 + tl
                        rows0 = hh * FFN_TQ + tt * 128
                        bp = tl % 2
                        p.ev("sync", lambda e, rows0=rows0, q=q, bp=bp: e.dma_start(
                            out=xlT[bp][:], in_=AP(spX, rows0 * H + q * 512, [[H, 128], [1, 512]])),
                            waits=[("s_ve", None)], inc="s_xr", n=16)
                        p.ev("vector", lambda e, tl=tl, q=q, bp=bp: e.scalar_tensor_tensor(
                            out=l3buf[:, tl, q * 512:(q + 1) * 512], in0=xlT[bp][:],
                            scalar=1.0, in1=psum[tl][:], op0=ALU.mult, op1=ALU.add),
                            waits=[("s_mm", mmv), ("s_xr", None), ("s_st", None)], inc="s_ve")
                    p.ev("vector", lambda e: e.nop(), inc="s_ve")
                    p.ev("vector", lambda e: e.nop(), inc="s_ve")
                    for tl in range(4):
                        bank_free[tl] = ("s_ve", p.cnt["s_ve"])
                for tl in range(4):
                    p.ev("vector", lambda e, tl=tl: e.tensor_scalar(
                        out=sc3[:], in0=l3buf[:, tl, :], scalar1=1.0,
                        scalar2=0.0, op0=ALU.mult, op1=ALU.add, accum_out=lnsx[:, tl, 0:1]), inc="s_ve")
                    p.ev("vector", lambda e, tl=tl: e.scalar_tensor_tensor(
                        out=sc3[:],
                        in0=l3buf[:, tl, :], scalar=1.0, in1=l3buf[:, tl, :],
                        op0=ALU.mult, op1=ALU.mult, accum_out=lnsx[:, tl, 1:2]), inc="s_ve")
                for tl in range(4):
                    tt = ttg * 4 + tl
                    p.ev("scalar", lambda e, tl=tl: e.activation(
                        out=lnsy[:, tl, 0:1], in_=lnsx[:, tl, 0:1], func=AF.Identity,
                        scale=1.0 / H), waits=[("s_ve", None)], inc="s_ac")
                    p.ev("scalar", lambda e, tl=tl: e.activation(
                        out=lnsx[:, tl, 2:3], in_=lnsx[:, tl, 0:1], func=AF.Square,
                        scale=1.0 / H), inc="s_ac")
                    p.ev("gpsimd", lambda e, tl=tl: e.tensor_scalar(
                        out=lnsy[:, tl, 2:3], in0=lnsx[:, tl, 1:2], scalar1=1.0 / H,
                        scalar2=lnsx[:, tl, 2:3], op0=ALU.mult, op1=ALU.subtract),
                        waits=[("s_ac", None), ("s_ve", None)], inc="s_gp")
                    p.ev("scalar", lambda e, tl=tl: e.activation(out=lnsy[:, tl, 2:3], in_=lnsy[:, tl, 2:3], func=AF.Ln, bias=epsv[:]), waits=[("s_gp", None)], inc="s_ac")
                    p.ev("scalar", lambda e, tl=tl: e.activation(out=lnsy[:, tl, 2:3], in_=lnsy[:, tl, 2:3], func=AF.Exp, scale=-0.5), inc="s_ac")
                    p.ev("vector", lambda e, tl=tl: e.tensor_scalar(
                        out=yt[tl % 2][:], in0=l3buf[:, tl, :], scalar1=lnsy[:, tl, 0:1], scalar2=lnsy[:, tl, 2:3],
                        op0=ALU.subtract, op1=ALU.mult),
                        waits=[("s_ac", None), ("s_st", None)], inc="s_ve")
                    rows0 = hh * FFN_TQ + tt * 128
                    p.ev("sync", lambda e, tl=tl, rows0=rows0: e.dma_start(
                        out=y[rows0:rows0 + 128, :], in_=yt[tl % 2][:]),
                        waits=[("s_ve", None)], inc="s_st", n=16)

        if dbg:
            p.ev("sync", lambda e: e.dma_start(out=spST.ap(), in_=stlog[:]),
                 waits=[("s_ve", None)], inc="s_sp", n=16)
        p.ev("gpsimd", None, waits=[("s_st", None), ("s_sp", None)])

        @block.sync
        def _(e):
            p.emit("sync", e, sems)

        @block.gpsimd
        def _(e):
            p.emit("gpsimd", e, sems)

        @block.vector
        def _(e):
            p.emit("vector", e, sems)

        @block.scalar
        def _(e):
            p.emit("scalar", e, sems)

        @block.tensor
        def _(e):
            p.emit("tensor", e, sems)

    return nc


_NC_CACHE = {}


def _get_nc(dbg=False):
    if dbg not in _NC_CACHE:
        _NC_CACHE[dbg] = build_program(dbg=dbg)
    return _NC_CACHE[dbg]


def make_in_maps(inputs):
    x = np.ascontiguousarray(np.asarray(inputs["L0_attn_x"], dtype=np.float32))
    Wk = np.asarray(inputs["L0_attn_Wk"], dtype=np.float32)
    Wv = np.asarray(inputs["L0_attn_Wv"], dtype=np.float32)
    Wr = np.asarray(inputs["L0_attn_Wr"], dtype=np.float32)
    Wo = np.asarray(inputs["L0_attn_Wo"], dtype=np.float32)
    scale = np.asarray(inputs["L0_attn_ln_scale"], dtype=np.float32)
    bias = np.asarray(inputs["L0_attn_ln_bias"], dtype=np.float32)
    mk = np.asarray(inputs["L0_attn_mix_k"], dtype=np.float32)
    mv = np.asarray(inputs["L0_attn_mix_v"], dtype=np.float32)
    mr = np.asarray(inputs["L0_attn_mix_r"], dtype=np.float32)
    tf = np.asarray(inputs["L0_attn_time_first"], dtype=np.float32)
    td = np.asarray(inputs["L0_attn_time_decay"], dtype=np.float32)
    lut = np.asarray(inputs["L0_ffn_lut"], dtype=np.float32)
    g_idx = np.asarray(inputs["L0_ffn_gate_idx"], dtype=np.int32)
    u_idx = np.asarray(inputs["L0_ffn_up_idx"], dtype=np.int32)
    d_idx = np.asarray(inputs["L0_ffn_down_idx"], dtype=np.int32)

    vecs = np.stack([scale, bias, mk, mv, mr, tf, td]).reshape(7, HI, 128)
    vecs = np.ascontiguousarray(vecs.transpose(0, 2, 1))
    grep = np.ascontiguousarray(np.tile(scale.astype(ml_dtypes.bfloat16)[None, :], (128, 1)))
    brep = np.ascontiguousarray(np.tile(bias.astype(ml_dtypes.bfloat16)[None, :], (128, 1)))
    lutb = np.ascontiguousarray(np.tile(lut[None, :], (128, 1)))
    ramp0 = np.ascontiguousarray(np.tile(np.arange(TB, dtype=np.float32)[None, :], (128, 1)))
    zrow = np.zeros((1, H), np.float32)

    in_maps = []
    for c in range(NCORE):
        b = c % 4
        sl = slice(c * 256, (c + 1) * 256)
        in_maps.append({
            "xh": np.ascontiguousarray(np.concatenate([zrow, x[b]], axis=0)),
            "wsl": np.ascontiguousarray(np.stack([Wk[sl], Wv[sl], Wr[sl], Wo[sl]])),
            "vecs": vecs,
            "grep_in": grep,
            "brep_in": brep,
            "lutb_in": lutb,
            "maskv_in": np.zeros((128, 1), np.float32),
            "ramp0_in": ramp0,
            "gidx": np.ascontiguousarray(g_idx[:, sl]),
            "uidx": np.ascontiguousarray(u_idx[:, sl]),
            "didx": np.ascontiguousarray(d_idx[sl, :]),
        })
    return in_maps


def kernel(**inputs):
    in_maps = make_in_maps(inputs)
    nc = _get_nc(dbg=False)
    res = run_bass_kernel_spmd(nc, in_maps, core_ids=list(range(NCORE)))
    yf = np.empty((4, TPC, H), np.float32)
    for c in range(4):
        yf[c] = res.results[c]["y"]
    return yf
